# version 19
# baseline (speedup 1.0000x reference)
import numpy as np
import concourse.bass as bass
import concourse.mybir as mybir
from concourse.bass_utils import run_bass_kernel_spmd

F32 = mybir.dt.float32
BF16 = mybir.dt.bfloat16
Alu = mybir.AluOpType
Act = mybir.ActivationFunctionType
AX = mybir.AxisListType.X

D = 1024
LEVEL = 3
NO_QUANT = False
NO_QP2 = False
NCH = 8
HD = 64
ST = 512
NT = 4
EPS = 1e-5


def build_program(n_pre, n_own):
    nc = bass.Bass("TRN2", target_bir_lowering=False)
    NOWNTOK = n_own * ST
    NPRETOK = max(n_pre, 1) * ST
    NSUP = n_pre + n_own
    NCOL = NSUP * ST

    sems = {}

    def S(name):
        if name not in sems:
            sems[name] = [nc.alloc_semaphore(name), 0]
        return sems[name]

    def inc(instr, name, k=1):
        s = S(name)
        instr.then_inc(s[0], k)
        s[1] += k
        return (name, s[1])

    def wt(eng, tok):
        if tok is not None:
            eng.wait_ge(S(tok[0])[0], tok[1])

    def now(name):
        return (name, S(name)[1])

    x_own = nc.dram_tensor("x_own", [NOWNTOK, D], F32, kind="ExternalInput")
    x_pre = nc.dram_tensor("x_pre", [NPRETOK, D], F32, kind="ExternalInput")
    w_h = {n: nc.dram_tensor(n, [D, D], F32, kind="ExternalInput")
           for n in ("wq", "wk", "wv", "wo", "wf")}
    wg1_h = nc.dram_tensor("wg1", [D, HD], F32, kind="ExternalInput")
    wg2_h = nc.dram_tensor("wg2", [HD, D], F32, kind="ExternalInput")
    lnw_h = nc.dram_tensor("lnw", [128, NCH], F32, kind="ExternalInput")
    lnb_h = nc.dram_tensor("lnb", [128, NCH], F32, kind="ExternalInput")
    fbias_h = nc.dram_tensor("fbias", [D], F32, kind="ExternalInput")
    fa_h = nc.dram_tensor("fa", [D, 16], F32, kind="ExternalInput")
    fb_h = nc.dram_tensor("fb", [16, D], F32, kind="ExternalInput")
    trig_h = {n: nc.dram_tensor(n, [128, NCOL], F32, kind="ExternalInput")
              for n in ("trh", "trl", "tuh", "tw")}
    maskt_h = nc.dram_tensor("maskt", [128, 128], F32, kind="ExternalInput")
    p128_h = nc.dram_tensor("p128", [128, 128], F32, kind="ExternalInput")
    ident_h = nc.dram_tensor("ident", [128, 128], F32, kind="ExternalInput")
    out_h = nc.dram_tensor("out", [NOWNTOK, D], F32, kind="ExternalOutput")
    scr_h = nc.dram_tensor("scr", [1, 132], F32)

    A = nc.alloc_sbuf_tensor
    W = {n: A(n + "_sb", [128, NCH * D], BF16) for n in ("wq", "wk", "wv", "wo", "wf")}
    wg1_sb = A("wg1_sb", [128, NCH * HD], BF16)
    wg2_sb = A("wg2_sb", [64, D], BF16)
    fa_sb = A("fa_sb", [128, NCH * 16], BF16)
    fb_sb = A("fb_sb", [16, D], BF16)
    fbias_sb = A("fbias_sb", [1, D], BF16)
    lnw_f = A("lnw_f", [128, NCH], F32)
    lnwb = A("lnwb", [128, NCH], BF16)
    lnbb = A("lnbb", [128, NCH], BF16)
    c1r = A("c1row", [1, D], BF16)
    c2r = A("c2row", [1, D], BF16)
    ones1 = A("ones1", [1, 128], BF16)
    onescol = A("onescol", [128, 1], BF16)
    onesf = A("onesf", [1, 128], F32)
    eps_sb = A("eps_sb", [128, 1], F32)
    r1 = A("r1", [1, ST], BF16)
    rstdb = A("rstdb", [1, ST], BF16)
    maskt = A("maskt_s", [128, 128], BF16)
    p128 = A("p128_s", [128, 128], BF16)
    ident = A("ident_s", [128, 128], BF16)

    XS = [A("XAslab", [128, NT * D], F32), A("XBslab", [128, NT * D], F32)]
    UT = A("UTslab", [128, NCH * ST], BF16, is_transpose_dest=True)
    QTA = A("QTAslab", [128, NCH * ST], BF16)
    QTB = A("QTBslab", [128, NCH * ST], BF16)
    KT = A("KTslab", [128, NCH * ST], BF16, is_transpose_dest=True)
    VS = A("VSslab", [128, NT * D], BF16)
    GT = A("GTslab", [128, NCH * ST], BF16)
    GO, ACORE, TTs, TS = UT, KT, KT, VS
    KTM_r = A("KTM_ring", [128, 2 * D], BF16, is_transpose_dest=True)
    MSK = A("MSK", [128, 1024], BF16)
    S_acc = A("S_acc", [128, 512], F32)
    S_sb = A("S_sb", [128, 1024], BF16)
    rope_m1 = A("rope_m1", [128, ST], BF16)
    rope_m2 = A("rope_m2", [128, ST], BF16)

    sin_r = A("sin_ring", [128, ST], BF16)
    cos_r = A("cos_ring", [128, ST], BF16)
    tqa = A("tqa", [128, 256], F32)
    tqb = A("tqb", [128, 256], F32)
    tqc = A("tqc", [128, 256], F32)
    tqd = A("tqd", [128, 256], F32)
    s0_sb = A("s0_sb", [128, ST], BF16)
    c1_sb = A("c1_sb", [128, ST], BF16)

    qsc = [A("qsc0", [128, 256], F32), A("qsc1", [128, 256], F32)]
    qsg = [A("qsg0", [128, 256], F32), A("qsg1", [128, 256], F32)]
    qu = [A("qu0", [128, 256], F32), A("qu1", [128, 256], F32)]
    qt_ = A("qt", [128, 256], F32)
    qr = A("qr", [128, 256], F32)
    mx8 = A("mx8", [128, 20], F32)
    m1r = A("m1r", [1, 128], F32)
    mrow = A("mrow", [1, 2], F32)
    scl128 = A("scl128", [128, 2], F32)

    def ln_bufs(tag):
        return {"bn": A(tag + "_bn", [128, 12 * NT], F32),
                "mv": A(tag + "_mv", [128, 2 * NT], F32),
                "sq": A(tag + "_sq", [128, NT], F32),
                "rs": A(tag + "_rs", [128, NT], F32),
                "nm": A(tag + "_nm", [128, NT], F32)}
    L1, L3, L4, L5 = (ln_bufs(t) for t in ("l1", "l3", "l4", "l5"))
    zrow = A("zrow", [1, ST], F32)
    g1_sb = A("g1_sb", [64, ST], BF16)

    pb = [nc.alloc_psum_tensor(f"pb{i}", [128, 512], F32) for i in range(6)]
    TB = [nc.alloc_psum_tensor(f"tb{i}", [128, 1024], BF16) for i in range(2)]
    bw = [None] * 8
    br = [[] for _ in range(8)]
    ESEM = {"t": "SP", "v": "SV", "s": "SA", "g": "SG"}

    def bwb(eng, b):
        for tok in br[b]:
            wt(eng, tok)
        br[b].clear()

    def bwe(instr, b, e):
        bw[b] = inc(instr, ESEM[e])
        return bw[b]

    def brb(eng, b):
        if bw[b] is not None:
            wt(eng, bw[b])

    def bre(instr, b, e):
        tok = inc(instr, ESEM[e])
        br[b].append(tok)
        return tok

    def bwe2(instr, b1, b2, e):
        tok = inc(instr, ESEM[e])
        bw[b1] = tok
        bw[b2] = tok
        return tok

    def bre2(instr, b1, b2, e):
        tok = inc(instr, ESEM[e])
        br[b1].append(tok)
        br[b2].append(tok)
        return tok

    def wch(wsb, kc, o0, n):
        return wsb[:, kc * D + o0: kc * D + o0 + n]

    sdefer = []

    def s_defer(fn, esem, n=1):
        s = S(esem)
        toks = [(esem, s[1] + i + 1) for i in range(n)]
        s[1] += n
        sdefer.append((fn, toks))
        return toks

    def rinc(instr, tok, k=1):
        instr.then_inc(S(tok[0])[0], k)

    st = {"xload": {}, "slab_free": [None, None], "u_slot": None,
          "ktm_slot": [None, None], "h": {}, "sb_cur": 0, "s_war": None,
          "trig_dma": {},
          "sin_slot": None, "trig_scr": None, "s_chain": None,
          "wf_ready": None, "c12_ready": None, "sw": {}, "gt_free": None,
          "ut_free": None, "kt_free": None, "qt_free": None, "vs_free": None,
          "rope_last": None}

    with nc.Block() as block:
        def s_flush():
            pend = list(sdefer)
            del sdefer[:]
            for fn_, toks_ in pend:
                def mk(fn=fn_, toks=toks_):
                    @block.scalar
                    def _(s):
                        fn(s, toks)
                mk()

        @block.gpsimd
        def _(g):
            def ld(name, dst, src):
                sname = "SW_" + name
                g.dma_start(dst, src).then_inc(S(sname)[0], 16)
                S(sname)[1] += 16
                st["sw"][name] = (sname, S(sname)[1])

            def ld4(name, sb, h):
                sname = "SW_" + name
                for i in range(4):
                    g.dma_start(
                        bass.AP(sb, i * 2 * D, [[NCH * D, 128], [D, 2], [1, D]]),
                        bass.AP(h, i * 2 * 128 * D, [[D, 128], [128 * D, 2], [1, D]]),
                    ).then_inc(S(sname)[0], 16)
                    S(sname)[1] += 16
                st["sw"][name] = (sname, S(sname)[1])

            ld("p128", p128[:], p128_h.ap())
            ld("ident", ident[:], ident_h.ap())
            ld4("wk", W["wk"], w_h["wk"])
            ld4("wv", W["wv"], w_h["wv"])
            ld4("wq", W["wq"], w_h["wq"])
            ld("wg1", bass.AP(wg1_sb, 0, [[NCH * HD, 128], [HD, NCH], [1, HD]]),
               bass.AP(wg1_h, 0, [[HD, 128], [128 * HD, NCH], [1, HD]]))
            ld("wg2", wg2_sb[:], wg2_h.ap())
            ld("maskt", maskt[:], maskt_h.ap())
            ld4("wo", W["wo"], w_h["wo"])
            ld("lnw", lnw_f[:], lnw_h.ap())
            ld("lnwb", lnwb[:], lnw_h.ap())
            ld("lnbb", lnbb[:], lnb_h.ap())
            ld("fa", bass.AP(fa_sb, 0, [[NCH * 16, 128], [16, NCH], [1, 16]]),
               bass.AP(fa_h, 0, [[16, 128], [128 * 16, NCH], [1, 16]]))
            ld("fb", fb_sb[:], fb_h.ap())
            ld("fbias", fbias_sb[:], bass.AP(fbias_h, 0, [[D, 1], [1, D]]))
            g.memset(ones1[:], 1.0)
            g.memset(onescol[:], 1.0 / 1024.0)
            g.memset(onesf[:], 1.0)
            g.memset(eps_sb[:], EPS)
            g.memset(S_acc[:], 0.0)
            g.memset(QTA[64:128, :], 0.0)
            g.memset(QTB[0:64, :], 0.0)
            st["init"] = inc(g.memset(S_sb[:], 0.0), "SG")
            st["s_chain"] = st["init"]

        def emit_xload(gi):
            slab = XS[gi % 2]
            src = x_pre if gi < n_pre else x_own
            row0 = (gi if gi < n_pre else gi - n_pre) * ST

            sname = "SXs%d" % (gi % 2)

            @block.sync
            def _(sy):
                wt(sy, st["slab_free"][gi % 2])
                sy.dma_start(
                    bass.AP(slab, 0, [[NT * D, 128], [D, NT], [1, D]]),
                    bass.AP(src, row0 * D, [[D, 128], [128 * D, NT], [1, D]]),
                ).then_inc(S(sname)[0], 16)
                S(sname)[1] += 16
            st["xload"][gi] = (sname, S(sname)[1])

        emit_xload(0)

        @block.sync
        def _(sy):
            if NO_QUANT:
                return
            sy.dma_start(XS[1][:],
                         bass.AP(w_h["wf"], 0, [[4096, 128], [1, 4096]])
                         ).then_inc(S("SXP")[0], 16)
            S("SXP")[1] += 16

        @block.vector
        def _(v):
            if NO_QUANT:
                return
            v.wait_ge(S("SXP")[0], 16)
            for c in range(8):
                i1 = v.tensor_reduce(mx8[:, c:c + 1],
                                     XS[1][:, 512 * c:512 * c + 512], AX, Alu.max,
                                     apply_absolute_value=True)
                inc(i1, "QR")

        @block.sync
        def _(sy):
            if NO_QUANT:
                return
            sy.wait_ge(S("QR")[0], 8)
            sy.dma_start(XS[1][:],
                         bass.AP(w_h["wf"], 4096 * 128, [[4096, 128], [1, 4096]])
                         ).then_inc(S("SXP")[0], 16)
            S("SXP")[1] += 16

        @block.vector
        def _(v):
            if NO_QUANT:
                return
            v.wait_ge(S("SXP")[0], 32)
            for c in range(8):
                i1 = v.tensor_reduce(mx8[:, 8 + c:9 + c],
                                     XS[1][:, 512 * c:512 * c + 512], AX, Alu.max,
                                     apply_absolute_value=True)
                inc(i1, "QR")
            wt(v, now("QR"))
            inc(v.tensor_reduce(mx8[:, 16:17], mx8[:, 0:16], AX, Alu.max), "SV")
        red_tok = now("SV")
        if not NO_QUANT:
            st["slab_free"][1] = red_tok

        @block.gpsimd
        def _(g):
            if NO_QUANT:
                return
            wt(g, red_tok)
            g.dma_start(bass.AP(scr_h, 2, [[1, 1], [1, 128]]),
                        bass.AP(mx8, 16, [[20, 128], [1, 1]])).then_inc(S("SWB")[0], 16)
            S("SWB")[1] += 16
            g.wait_ge(S("SWB")[0], S("SWB")[1])
            g.dma_start(m1r[:], bass.AP(scr_h, 2, [[1, 1], [1, 128]])
                        ).then_inc(S("SWB")[0], 16)
            S("SWB")[1] += 16
        bounce_tok = ("SWB", S("SWB")[1])

        @block.vector
        def _(v):
            if NO_QUANT:
                return
            wt(v, bounce_tok)
            t = inc(v.tensor_reduce(mrow[0:1, 0:1], m1r[:], AX, Alu.max), "SV")
            wt(v, t)
            t = inc(v.reciprocal(mrow[0:1, 1:2], mrow[0:1, 0:1]), "SV")
            wt(v, t)
            t = inc(v.tensor_scalar(mrow[0:1, 0:1], mrow[0:1, 0:1],
                                    0.14285714285714285, None, Alu.mult), "SV")
            wt(v, t)
            inc(v.tensor_scalar(mrow[0:1, 1:2], mrow[0:1, 1:2], 7.0, None,
                                Alu.mult), "SV")
        mrow_tok = now("SV")

        @block.tensor
        def _(t):
            if NO_QUANT:
                return
            wt(t, mrow_tok)
            wt(t, st["init"])
            bwb(t, 0)
            i1 = t.matmul(pb[0][:, 0:2], onesf[:], mrow[:], start=True, stop=True)
            bwe(i1, 0, "t")

        @block.vector
        def _(v):
            if NO_QUANT:
                return
            brb(v, 0)
            i1 = v.tensor_copy(scl128[:], pb[0][:, 0:2])
            bre(i1, 0, "v")
        scl_tok = now("SV")

        qp2 = {"loads": 0, "done": 0}

        def emit_qp2_loads(n):
            @block.gpsimd
            def _(sy):
                for _i in range(n):
                    i = qp2["loads"]
                    if i >= 32:
                        return
                    kc, qr_ = i // 4, i % 4
                    if i >= 2:
                        sy.wait_ge(S("QA")[0], 2 * (i - 1))
                    sq = "SXQ2%d" % (i % 2)
                    sy.dma_start(qsc[i % 2][:],
                                 bass.AP(w_h["wf"], kc * 128 * D + qr_ * 256,
                                         [[D, 128], [1, 256]])
                                 ).then_inc(S(sq)[0], 16)
                    S(sq)[1] += 16
                    qp2["loads"] += 1

        def emit_qp2_ladder(n):
            for _i in range(n):
                i = qp2["done"]
                if i >= 32:
                    return
                kc, qr_ = i // 4, i % 4
                w = qsc[i % 2]

                def _lad(s, toks, i=i, w=w):
                    s.wait_ge(S("SXQ2%d" % (i % 2))[0], 16 * (i // 2 + 1))
                    if i == 0:
                        wt(s, scl_tok)
                    if i >= 2:
                        s.wait_ge(S("QD")[0], i - 1)
                    s.activation(qu[i % 2][:], w[:], Act.Abs, scale=scl128[:, 1:2])
                    i1 = s.activation(qsg[i % 2][:], w[:], Act.Sign)
                    rinc(i1, toks[0], 2)
                s_defer(_lad, "QA", 2)

                @block.vector
                def _(g):
                    def gs(i1):
                        tok = inc(i1, "SV")
                        wt(g, tok)
                    g.wait_ge(S("QA")[0], 2 * (i + 1))
                    gs(g.tensor_scalar(qt_[:], qu[i % 2][:], 3.5, None, Alu.is_ge))
                    gs(g.scalar_tensor_tensor(qr[:], qt_[:], -4.0, qu[i % 2][:],
                                              Alu.mult, Alu.add))
                    gs(g.tensor_scalar(qt_[:], qr[:], 1.5, None, Alu.is_ge))
                    gs(g.scalar_tensor_tensor(qr[:], qt_[:], -2.0, qr[:],
                                              Alu.mult, Alu.add))
                    gs(g.tensor_scalar(qt_[:], qr[:], 0.5, None, Alu.is_ge))
                    gs(g.scalar_tensor_tensor(qr[:], qt_[:], -1.0, qr[:],
                                              Alu.mult, Alu.add))
                    gs(g.tensor_tensor(qr[:], qu[i % 2][:], qr[:], Alu.subtract))
                    gs(g.tensor_tensor(qr[:], qr[:], qsg[i % 2][:], Alu.mult))
                    i1 = g.tensor_scalar(
                        W["wf"][:, kc * D + qr_ * 256: kc * D + qr_ * 256 + 256],
                        qr[:], scl128[:, 0:1], None, Alu.mult)
                    inc(i1, "QD")
                qp2["done"] += 1
                if qp2["done"] == 32:
                    st["wf_ready"] = ("QD", 32)
            s_flush()

        def emit_c12():
            for row in range(4):
                vec = lnwb if row < 2 else lnbb
                dst = c1r if row < 2 else c2r
                o0 = 512 * (row % 2)

                cb = 4 + row % 2

                @block.tensor
                def _(t):
                    if row == 0:
                        wt(t, st["sw"]["wo"])
                        wt(t, st["sw"]["lnwb"])
                        wt(t, st["sw"]["lnbb"])
                    bwb(t, cb)
                    last = None
                    for kc in range(NCH):
                        last = t.matmul(pb[cb][0:1, 0:512],
                                        vec[:, kc:kc + 1], wch(W["wo"], kc, o0, 512),
                                        start=(kc == 0), stop=(kc == NCH - 1))
                    bwe(last, cb, "t")

                @block.vector
                def _(v):
                    brb(v, cb)
                    i1 = v.tensor_copy(dst[0:1, o0:o0 + 512], pb[cb][0:1, 0:512])
                    bre(i1, cb, "v")
            st["c12_ready"] = now("SV")

        def ln_stats_slab(L, slab, pre_tok):
            @block.vector
            def _(v):
                wt(v, pre_tok)
                last = None
                for t in range(NT):
                    v.bn_stats(L["bn"][:, 12 * t:12 * t + 6],
                               slab[:, 1024 * t:1024 * t + 512])
                    last = v.bn_stats(L["bn"][:, 12 * t + 6:12 * t + 12],
                                      slab[:, 1024 * t + 512:1024 * t + 1024])
                tok = inc(last, "SV")
                wt(v, tok)
                for t in range(NT):
                    last = v.bn_aggr(L["mv"][:, 2 * t:2 * t + 2],
                                     L["bn"][:, 12 * t:12 * t + 12])
                inc(last, "SV")
            stats_tok = now("SV")

            @block.scalar
            def _(s):
                wt(s, stats_tok)
                i1 = s.activation(L["sq"][:],
                                  bass.AP(L["mv"], 1, [[2 * NT, 128], [2, NT]]),
                                  Act.Ln, bias=eps_sb[:, 0:1])
                tok = inc(i1, "SA")
                wt(s, tok)
                i1 = s.activation(L["rs"][:], L["sq"][:], Act.Exp, scale=-0.5)
                inc(i1, "SA")
            sq_tok = now("SA")

            @block.vector
            def _(v):
                wt(v, sq_tok)
                i1 = v.scalar_tensor_tensor(
                    L["nm"][:], bass.AP(L["mv"], 0, [[2 * NT, 128], [2, NT]]),
                    -1.0, L["rs"][:], Alu.mult, Alu.mult)
                inc(i1, "SV")
            return now("SV")

        def ln_stats_banks(L, t, bA, bB):
            @block.vector
            def _(v):
                brb(v, bA)
                brb(v, bB)
                v.bn_stats(L["bn"][:, 12 * t:12 * t + 6], pb[bA][:, 0:512])
                i1 = v.bn_stats(L["bn"][:, 12 * t + 6:12 * t + 12], pb[bB][:, 0:512])
                tok = bre2(i1, bA, bB, "v")
                wt(v, tok)
                i1 = v.bn_aggr(L["mv"][:, 2 * t:2 * t + 2],
                               L["bn"][:, 12 * t:12 * t + 12])
                inc(i1, "SV")
            stats_tok = now("SV")

            def _sq(s, toks, L=L, t=t, stats_tok=stats_tok):
                wt(s, stats_tok)
                i1 = s.activation(L["sq"][:, t:t + 1], L["mv"][:, 2 * t + 1:2 * t + 2],
                                  Act.Ln, bias=eps_sb[:, 0:1])
                rinc(i1, toks[0])
                wt(s, toks[0])
                i1 = s.activation(L["rs"][:, t:t + 1], L["sq"][:, t:t + 1],
                                  Act.Exp, scale=-0.5)
                rinc(i1, toks[1])
            sq_tok = s_defer(_sq, "SQB", 2)[1]

            @block.vector
            def _(v):
                wt(v, sq_tok)
                i1 = v.scalar_tensor_tensor(L["nm"][:, t:t + 1],
                                            L["mv"][:, 2 * t:2 * t + 1], -1.0,
                                            L["rs"][:, t:t + 1], Alu.mult, Alu.mult)
                inc(i1, "SV")
            return now("SV")

        def ln_banks_apply(L, t, bA, bB):
            @block.vector
            def _(v):
                brb(v, bA)
                brb(v, bB)
                v.bn_stats(L["bn"][:, 12 * t:12 * t + 6], pb[bA][:, 0:512])
                i1 = v.bn_stats(L["bn"][:, 12 * t + 6:12 * t + 12], pb[bB][:, 0:512])
                tok = bre2(i1, bA, bB, "v")
                wt(v, tok)
                i1 = v.bn_aggr(L["mv"][:, 2 * t:2 * t + 2],
                               L["bn"][:, 12 * t:12 * t + 12])
                inc(i1, "SV")
            stats_tok = now("SV")
            cell = [None]
            wA, wB = bw[bA], bw[bB]

            def _sqap(s, toks, L=L, t=t, bA=bA, bB=bB, stats_tok=stats_tok,
                      wA=wA, wB=wB):
                wt(s, stats_tok)
                i1 = s.activation(L["sq"][:, t:t + 1],
                                  L["mv"][:, 2 * t + 1:2 * t + 2],
                                  Act.Ln, bias=eps_sb[:, 0:1])
                rinc(i1, toks[0])
                wt(s, toks[0])
                i1 = s.activation(L["rs"][:, t:t + 1], L["sq"][:, t:t + 1],
                                  Act.Exp, scale=-0.5)
                rinc(i1, toks[1])
                wt(s, cell[0])
                wt(s, wA)
                wt(s, wB)
                s.activation(pb[bA][:, 0:512], pb[bA][:, 0:512], Act.Identity,
                             bias=L["nm"][:, t:t + 1], scale=L["rs"][:, t:t + 1])
                i1 = s.activation(pb[bB][:, 0:512], pb[bB][:, 0:512], Act.Identity,
                                  bias=L["nm"][:, t:t + 1],
                                  scale=L["rs"][:, t:t + 1])
                rinc(i1, toks[2])
            toks = s_defer(_sqap, "SQB", 3)
            sq_tok, ap_done = toks[1], toks[2]

            @block.vector
            def _(v):
                wt(v, sq_tok)
                i1 = v.scalar_tensor_tensor(L["nm"][:, t:t + 1],
                                            L["mv"][:, 2 * t:2 * t + 1], -1.0,
                                            L["rs"][:, t:t + 1], Alu.mult, Alu.mult)
                cell[0] = inc(i1, "SV")
            bw[bA] = ap_done
            bw[bB] = ap_done
            return ap_done

        def emit_trig_loads(g, hw, ab_tok=None):
            c0 = g * ST + 256 * hw

            @block.sync
            def _(sy):
                wt(sy, ab_tok if ab_tok is not None else st["trig_scr"])
                for n, buf in (("trh", tqa), ("tuh", tqb)):
                    sy.dma_start(buf[:],
                                 bass.AP(trig_h[n], c0, [[NCOL, 128], [1, 256]])
                                 ).then_inc(S("SXTA")[0], 16)
                    S("SXTA")[1] += 16
                wt(sy, st["trig_scr"])
                for n, buf in (("trl", tqc), ("tw", tqd)):
                    sy.dma_start(buf[:],
                                 bass.AP(trig_h[n], c0, [[NCOL, 128], [1, 256]])
                                 ).then_inc(S("SXTA")[0], 16)
                    S("SXTA")[1] += 16
            st["trig_dma"][g] = ("SXTA", S("SXTA")[1])

        def emit_head_front(g):
            slab = XS[g % 2]
            col0 = g * ST
            h = st["h"][g] = {}

            if LEVEL == -4:
                h["trunc"] = st["xload"][g]
                return


            h["ln1"] = ln_stats_slab(L1, slab, st["xload"][g])
            if LEVEL == -23:
                h["trunc"] = h["ln1"]

        def emit_head_trig(g):
            h = st["h"][g]
            if LEVEL == -4:
                return
            for hw in range(2):
                trig_dma = st["trig_dma"].pop(g, None)
                if trig_dma is None:
                    emit_trig_loads(g, hw, ab_tok=st.get("trig_sin"))
                    trig_dma = st["trig_dma"].pop(g)
                sl2 = slice(256 * hw, 256 * hw + 256)

                @block.scalar
                def _(s):
                    wt(s, trig_dma)
                    s.activation(s0_sb[:, sl2], tqa[:], Act.Sin)
                    i1 = s.activation(c1_sb[:, sl2], tqb[:], Act.Sin)
                    inc(i1, "SA")
                s0c1_tok = now("SA")
                st["trig_sin"] = s0c1_tok

                @block.vector
                def _(v):
                    wt(v, s0c1_tok)
                    if hw == 0:
                        wt(v, st["sin_slot"])
                    tok = inc(v.tensor_tensor(tqc[:], tqc[:], c1_sb[:, sl2],
                                              Alu.mult), "SV")
                    wt(v, tok)
                    tok = inc(v.tensor_tensor(sin_r[:, sl2], s0_sb[:, sl2],
                                              tqc[:], Alu.add), "SV")
                    wt(v, tok)
                    tok = inc(v.tensor_tensor(tqd[:], tqd[:], s0_sb[:, sl2],
                                              Alu.mult), "SV")
                    wt(v, tok)
                    i1 = v.tensor_tensor(cos_r[:, sl2], c1_sb[:, sl2],
                                         tqd[:], Alu.subtract)
                    st["trig_scr"] = inc(i1, "SV")
            trig_tok = st["trig_scr"]
            h["trig"] = trig_tok
            if LEVEL == -3:
                h["trunc"] = trig_tok

        def emit_head_back(g):
            h = st["h"][g]
            if "trunc" in h:
                return
            slab = XS[g % 2]
            ln1_tok = h["ln1"]
            gt_last = st["gt_free"]
            u_tok = [None] * NT
            u_evac = [None] * NT

            for t in range(NT):
                def _apu(s, toks, t=t):
                    if t == 0:
                        wt(s, ln1_tok)
                        wt(s, gt_last)
                    i1 = s.activation(GT[:, 1024 * t:1024 * t + 1024],
                                      slab[:, 1024 * t:1024 * t + 1024], Act.Identity,
                                      bias=L1["nm"][:, t:t + 1],
                                      scale=L1["rs"][:, t:t + 1])
                    rinc(i1, toks[0])
                [u_tok[t]] = s_defer(_apu, "SL")
                if t == NT - 1:
                    s_flush()

                if LEVEL == -22:
                    continue

                @block.tensor
                def _(te):
                    wt(te, u_tok[t])
                    if t == 0:
                        wt(te, st["sw"]["ident"])
                    for half in range(2):
                        b = 6 + half
                        bwb(te, b)
                        for ch4 in range(4):
                            ch = 4 * half + ch4
                            i1 = te.transpose(
                                TB[half][:, 128 * ch4:128 * ch4 + 128],
                                GT[:, 1024 * t + 128 * ch:1024 * t + 128 * ch + 128],
                                ident[:])
                        bwe(i1, b, "t")
                    st["gt_free"] = now("SP")

                @block.vector
                def _(v):
                    if t == 0:
                        wt(v, st["ut_free"])
                    for half in range(2):
                        b = 6 + half
                        brb(v, b)
                        i1 = v.tensor_copy(
                            bass.AP(UT, (4 * half) * ST + 128 * t,
                                    [[NCH * ST, 128], [ST, 4], [1, 128]]),
                            TB[half][:, 0:512])
                        bre(i1, b, "v")
                    u_evac[t] = now("SV")
            h["slab_last"] = u_tok[NT - 1]
            if LEVEL == -22:
                h["trunc"] = u_tok[NT - 1]
                return
            h["ut"] = [u_evac[t] for t in range(NT)]

        def emit_head(g):
            emit_head_front(g)
            emit_head_trig(g)
            emit_head_back(g)

        def emit_body(g):
            own = g >= n_pre
            slab = XS[g % 2]
            sl = g % 2
            h = st["h"][g]

            def dbg_store(tok):
                if not own:
                    st["slab_free"][g % 2] = tok
                    return
                sn_store = "SO%d" % (g % 2)

                @block.gpsimd
                def _(sy):
                    wt(sy, tok)
                    row0 = (g - n_pre) * ST
                    sy.dma_start(
                        bass.AP(out_h, row0 * D, [[D, 128], [128 * D, NT], [1, D]]),
                        bass.AP(slab, 0, [[NT * D, 128], [D, NT], [1, D]]),
                    ).then_inc(S(sn_store)[0], 16)
                    S(sn_store)[1] += 16
                st["slab_free"][g % 2] = (sn_store, S(sn_store)[1])

            if "trunc" in h:
                dbg_store(h["trunc"])
                return

            if g + 1 < NSUP:
                emit_xload(g + 1)
                if LEVEL == 3:
                    emit_trig_loads(g + 1, 0)
            trig_tok = h["trig"]
            ut_tok = h["ut"]

            if LEVEL == -2:
                st["ut_free"] = ut_tok[NT - 1]
                dbg_store(ut_tok[NT - 1])
                return

            def proj_rope(dst, wname, free_tok, qsplit=False):
                last_p3 = st["rope_last"]
                silu_toks = [None] * NCH
                for i_ in range(NCH + 4):
                  if i_ < NCH:
                    oc = i_
                    b = oc % 4

                    @block.tensor
                    def _(t):
                        if oc == 0:
                            wt(t, st["sw"][wname])
                            wt(t, st["sw"]["p128"])
                            for tk in ut_tok:
                                wt(t, tk)
                        bwb(t, b)
                        for kc in range(NCH):
                            i1 = t.matmul(pb[b][:, 0:512],
                                          wch(W[wname], kc, 128 * oc, 128),
                                          UT[:, kc * ST:(kc + 1) * ST],
                                          start=(kc == 0), stop=(kc == NCH - 1))
                        bwe(i1, b, "t")

                    def _silu(s, toks, oc=oc, b=b, pw=bw[b],
                              ft=(free_tok if oc == 0 else None)):
                        if ft is not None:
                            wt(s, ft)
                        wt(s, pw)
                        i1 = s.activation(dst[:, oc * ST:(oc + 1) * ST],
                                          pb[b][:, 0:512], Act.Silu)
                        rinc(i1, toks[0])
                    [silu_toks[oc]] = s_defer(_silu, "SL")
                    br[b].append(silu_toks[oc])

                  if i_ >= 4:
                    oc = i_ - 4
                    if True:
                        rb = 4 + oc % 2
                        silu_tok = silu_toks[oc]

                    @block.tensor
                    def _(t):
                        wt(t, silu_tok)
                        bwb(t, rb)
                        i1 = t.matmul(pb[rb][:, 0:512], p128[:],
                                      dst[:, oc * ST:(oc + 1) * ST],
                                      start=True, stop=True)
                        bwe(i1, rb, "t")
                    rot_tok = now("SP")

                    lp = last_p3

                    @block.vector
                    def _(v):
                        if oc == 0:
                            wt(v, trig_tok)
                        wt(v, lp)
                        wt(v, silu_tok)
                        tok = inc(v.tensor_tensor(
                            rope_m1[:], dst[:, oc * ST:(oc + 1) * ST],
                            cos_r[:, 0:ST], Alu.mult), "SV")
                        brb(v, rb)
                        i2 = v.tensor_tensor(rope_m2[:], pb[rb][:, 0:512],
                                             sin_r[:, 0:ST], Alu.mult)
                        tok2 = bre(i2, rb, "v")
                        wt(v, tok)
                        wt(v, tok2)
                        if not qsplit:
                            i3 = v.tensor_tensor(dst[:, oc * ST:(oc + 1) * ST],
                                                 rope_m1[:], rope_m2[:], Alu.add)
                            inc(i3, "SV")
                        else:
                            v.tensor_tensor(QTA[0:64, oc * ST:(oc + 1) * ST],
                                            rope_m1[0:64, :], rope_m2[0:64, :],
                                            Alu.add)
                            tok3 = inc(v.tensor_tensor(
                                QTB[64:128, oc * ST:(oc + 1) * ST],
                                rope_m1[64:128, :], rope_m2[64:128, :], Alu.add), "SV")
                            wt(v, tok3)
                            wt(v, rot_tok)
                            i3 = v.memset(QTB[0:64, oc * ST:(oc + 1) * ST], 0.0)
                            inc(i3, "SV")
                    last_p3 = now("SV")
                s_flush()
                st["rope_last"] = last_p3
                return last_p3

            kt_rope = proj_rope(KT, "wk", st["kt_free"])
            if own and LEVEL >= 1:
                qt_rope = proj_rope(QTB, "wq", st["qt_free"], qsplit=True)
            st["sin_slot"] = now("SV")
            if LEVEL == 3 and not (NO_QUANT or NO_QP2):
                emit_qp2_ladder(2)

            if LEVEL == -1:
                st["ut_free"] = now("SP")
                st["kt_free"] = st["rope_last"]
                st["qt_free"] = now("SP")
                dbg_store(st["rope_last"])
                return

            ktm_tok = [None] * NT

            def emit_ktm(t):
                tb = 6 + t % 2

                @block.tensor
                def _(te):
                    if t == 0:
                        wt(te, kt_rope)
                    wt(te, st["ktm_slot"][t % 2])
                    bwb(te, tb)
                    for ch in range(NCH):
                        i1 = te.transpose(
                            TB[t % 2][:, 128 * ch:128 * ch + 128],
                            KT[:, ch * ST + 128 * t: ch * ST + 128 * t + 128],
                            ident[:])
                    bwe(i1, tb, "t")

                @block.vector
                def _(v):
                    brb(v, tb)
                    i1 = v.tensor_copy(
                        bass.AP(KTM_r, (t % 2) * D, [[2 * D, 128], [1, 1024]]),
                        TB[t % 2][:, 0:1024])
                    bre(i1, tb, "v")
                ktm_tok[t] = now("SV")

            v_tok = [None] * NT
            for t_ in range(NT):
                vA, vB = (2, 3) if t_ % 2 == 0 else (0, 1)

                @block.tensor
                def _(t):
                    if t_ == 0:
                        wt(t, st["sw"]["wv"])
                    bwb(t, vA)
                    bwb(t, vB)
                    for kc in range(NCH):
                        lhs = UT[:, kc * ST + 128 * t_: kc * ST + 128 * t_ + 128]
                        t.matmul(pb[vA][:, 0:512], lhs, wch(W["wv"], kc, 0, 512),
                                 start=(kc == 0), stop=(kc == NCH - 1))
                        i1 = t.matmul(pb[vB][:, 0:512], lhs,
                                      wch(W["wv"], kc, 512, 512),
                                      start=(kc == 0), stop=(kc == NCH - 1))
                    bwe2(i1, vA, vB, "t")

                @block.vector
                def _(v):
                    if t_ == 0:
                        wt(v, st["vs_free"])
                    brb(v, vA)
                    brb(v, vB)
                    v.tensor_copy(VS[:, 1024 * t_:1024 * t_ + 512], pb[vA][:, 0:512])
                    i2 = v.tensor_copy(VS[:, 1024 * t_ + 512:1024 * t_ + 1024],
                                       pb[vB][:, 0:512])
                    bre2(i2, vA, vB, "v")
                v_tok[t_] = now("SV")

            if g == min(0, NSUP - 1) and LEVEL != -4 and not (NO_QUANT
                                                               or NO_QP2):
                emit_c12()
            if LEVEL == 3 and not (NO_QUANT or NO_QP2):
                emit_qp2_ladder(2)

            if not own:
                st["ut_free"] = now("SP")

            if own and LEVEL >= 1:
                @block.tensor
                def _(t):
                    wt(t, st["sw"]["wg1"])
                    wt(t, st["sw"]["wg2"])
                    bwb(t, 5)
                    for kc in range(NCH):
                        i1 = t.matmul(pb[5][0:64, 0:512],
                                      wg1_sb[:, kc * HD:(kc + 1) * HD],
                                      UT[:, kc * ST:(kc + 1) * ST],
                                      start=(kc == 0), stop=(kc == NCH - 1))
                    bwe(i1, 5, "t")

                @block.vector
                def _(v):
                    brb(v, 5)
                    i1 = v.tensor_copy(g1_sb[:], pb[5][0:64, 0:512])
                    bre(i1, 5, "v")
                g1_tok = now("SV")

                for oc in range(NCH):
                    b = oc % 2

                    @block.tensor
                    def _(t):
                        if oc == 0:
                            wt(t, g1_tok)
                        bwb(t, b)
                        i1 = t.matmul(pb[b][:, 0:512],
                                      wg2_sb[:, 128 * oc:128 * oc + 128],
                                      g1_sb[:], start=True, stop=True)
                        bwe(i1, b, "t")

                    def _sig(s, toks, oc=oc, b=b, pw=bw[b],
                             ft=(st["gt_free"] if oc == 0 else None)):
                        if ft is not None:
                            wt(s, ft)
                        wt(s, pw)
                        i2 = s.activation(GT[:, oc * ST:(oc + 1) * ST],
                                          pb[b][:, 0:512], Act.Sigmoid)
                        rinc(i2, toks[0])
                    [sg_tok] = s_defer(_sig, "SL")
                    br[b].append(sg_tok)
                if True:
                    s_flush()

            go_tok = None
            for t in range(NT):
                emit_ktm(t)
                if own and LEVEL >= 1 and LEVEL != 11:
                    msk_tok = [None] * 8
                    ou_base = S("OU")[1]
                    out_tok = [None] * 2
                    ou_tok = [None] * 8
                    cur = 512 * st["sb_cur"]

                    def emit_out(p):
                        if LEVEL == 12:
                            @block.tensor
                            def _(te):
                                wt(te, msk_tok[p])
                                ou_tok[p] = inc(te.nop(), "OU")
                            if p % 4 == 3:
                                out_tok[p // 4] = ou_tok[p]
                            return
                        ob = 4 + p // 4
                        col = 128 * (p % 4)

                        @block.tensor
                        def _(te):
                            if p % 4 == 0:
                                bwb(te, ob)
                            wt(te, msk_tok[p])
                            te.matmul(pb[ob][0:64, col:col + 128],
                                      VS[:, 1024 * t + 128 * p:
                                         1024 * t + 128 * p + 64],
                                      MSK[:, 256 * (p % 4):256 * (p % 4) + 128],
                                      start=True, stop=False, tile_position=(0, 0))
                            if p == 0:
                                wt(te, st["s_chain"])
                            te.matmul(pb[ob][0:64, col:col + 128],
                                      S_sb[:, cur + 64 * p:cur + 64 * p + 64],
                                      QTA[:, p * ST + 128 * t: p * ST + 128 * t + 128],
                                      start=False, stop=True, tile_position=(0, 0))
                            te.matmul(pb[ob][64:128, col:col + 128],
                                      VS[:, 1024 * t + 128 * p + 64:
                                         1024 * t + 128 * p + 128],
                                      MSK[:, 256 * (p % 4) + 128:256 * (p % 4) + 256],
                                      start=True, stop=False, tile_position=(0, 64))
                            i1 = te.matmul(
                                pb[ob][64:128, col:col + 128],
                                S_sb[:, cur + 64 * p:cur + 64 * p + 64],
                                QTB[:, p * ST + 128 * t: p * ST + 128 * t + 128],
                                start=False, stop=True, tile_position=(0, 64))
                            ou_tok[p] = inc(i1, "OU")
                            if p % 4 == 3:
                                out_tok[p // 4] = ou_tok[p]
                                bw[ob] = ou_tok[p]

                    for p in range(8):
                        b = p % 3

                        @block.tensor
                        def _(te):
                            if t == 0 and p == 0:
                                wt(te, kt_rope)
                                wt(te, qt_rope)
                            bwb(te, b)
                            kl = KT[:, p * ST + 128 * t: p * ST + 128 * t + 128]
                            te.matmul(pb[b][:, 0:128], kl,
                                      QTA[:, p * ST + 128 * t: p * ST + 128 * t + 128],
                                      start=True, stop=True)
                            i1 = te.matmul(
                                pb[b][:, 128:256], kl,
                                QTB[:, p * ST + 128 * t: p * ST + 128 * t + 128],
                                start=True, stop=True)
                            bwe(i1, b, "t")

                        @block.vector
                        def _(v):
                            if t == 0 and p == 0 and g == n_pre:
                                wt(v, st["sw"]["maskt"])
                            if p >= 4:
                                wt(v, ("OU", ou_base + (p - 4) + 1))
                            brb(v, b)
                            i1 = v.tensor_tensor(
                                MSK[:, 256 * (p % 4):256 * (p % 4) + 256],
                                pb[b][:, 0:256],
                                bass.AP(maskt, 0, [[128, 128], [0, 2], [1, 128]]),
                                Alu.mult)
                            bre(i1, b, "v")
                        msk_tok[p] = now("SV")
                        if p >= 3:
                            emit_out(p - 3)

                @block.tensor
                def _(te):
                    if not own and t == 0:
                        wt(te, kt_rope)
                    wt(te, ktm_tok[t])
                    wt(te, v_tok[t])
                    bwb(te, 3)
                    for p in range(8):
                        te.matmul(pb[3][0:64, 64 * p:64 * p + 64],
                                  KTM_r[:, (t % 2) * D + 128 * p:
                                        (t % 2) * D + 128 * p + 64],
                                  VS[:, 1024 * t + 128 * p:1024 * t + 128 * p + 64],
                                  start=True, stop=True, tile_position=(0, 0))
                        i1 = te.matmul(pb[3][64:128, 64 * p:64 * p + 64],
                                       KTM_r[:, (t % 2) * D + 128 * p + 64:
                                             (t % 2) * D + 128 * p + 128],
                                       VS[:, 1024 * t + 128 * p + 64:
                                          1024 * t + 128 * p + 128],
                                       start=True, stop=True,
                                       tile_position=(0, 64))
                    st["ktm_slot"][t % 2] = bwe(i1, 3, "t")

                if own and LEVEL >= 1 and LEVEL != 11:
                    for p in range(5, 8):
                        emit_out(p)

                pre_final = (not own) and g == n_pre - 1 and t == NT - 1
                s_war = st["s_war"] if own else st["init"]

                @block.vector
                def _(v):
                    brb(v, 3)
                    i_add = v.tensor_tensor(S_acc[:], S_acc[:], pb[3][:, 0:512],
                                            Alu.add)
                    if own or pre_final:
                        tok = inc(i_add, "SV")
                        wt(v, tok)
                        wt(v, s_war)
                        nxt = 0 if pre_final else 1 - st["sb_cur"]
                        i1 = v.tensor_copy(S_sb[:, 512 * nxt:512 * nxt + 512],
                                           S_acc[:])
                        bre(i1, 3, "v")
                        st["sb_cur"] = nxt
                    else:
                        bre(i_add, 3, "v")
                if own or pre_final:
                    st["s_chain"] = now("SV")
                if own:
                    st["s_war"] = ou_tok[7] if LEVEL >= 1 and LEVEL != 11 else None

                if own and LEVEL >= 1 and LEVEL not in (11, 12):
                    @block.vector
                    def _(v):
                        for half in range(2):
                            ob = 4 + half
                            brb(v, ob)
                            i1 = v.tensor_tensor(
                                bass.AP(GO, 512 * 4 * half + 128 * t,
                                        [[NCH * ST, 128], [ST, 4], [1, 128]]),
                                pb[ob][:, 0:512],
                                bass.AP(GT, 512 * 4 * half + 128 * t,
                                        [[NCH * ST, 128], [ST, 4], [1, 128]]),
                                Alu.mult)
                            bre(i1, ob, "v")
                    go_tok = now("SV")

                if not own and t == 1 and LEVEL == 3 and g + 1 < NSUP:
                    emit_head_front(g + 1)
                    emit_head_trig(g + 1)
                if t in (1, 3) and LEVEL == 3 and not (NO_QUANT or NO_QP2):
                    emit_qp2_ladder(2)

            def early_store(tok):
                sn_store = "SO%d" % (g % 2)

                @block.gpsimd
                def _(sy):
                    wt(sy, tok)
                    row0 = (g - n_pre) * ST
                    sy.dma_start(
                        bass.AP(out_h, row0 * D, [[D, 128], [128 * D, NT], [1, D]]),
                        bass.AP(slab, 0, [[NT * D, 128], [D, NT], [1, D]]),
                    ).then_inc(S(sn_store)[0], 16)
                    S(sn_store)[1] += 16
                st["slab_free"][g % 2] = (sn_store, S(sn_store)[1])
                st["ut_free"] = now("SP")
                st["kt_free"] = ktm_tok[NT - 1]
                st["qt_free"] = now("SP")
                st["vs_free"] = st["ktm_slot"][(NT - 1) % 2]
                if own and go_tok is not None:
                    st["gt_free"] = go_tok

            if not own:
                st["slab_free"][g % 2] = h["slab_last"]
                st["kt_free"] = ktm_tok[NT - 1]
                st["vs_free"] = st["ktm_slot"][(NT - 1) % 2]
                if LEVEL == 3 and g + 1 < NSUP:
                    emit_head_back(g + 1)
                return

            if LEVEL <= 1:
                early_store(st["s_chain"])
                return

            st["gt_free"] = go_tok

            @block.tensor
            def _(te):
                wt(te, go_tok)
                bwb(te, 4)
                bwb(te, 5)
                for p in range(NCH):
                    i1 = te.matmul(pb[4][0:1, 0:512], onescol[:],
                                   GO[:, p * ST:(p + 1) * ST],
                                   start=(p == 0), stop=(p == NCH - 1))
                bwe(i1, 4, "t")

            sqslot = [rope_m1[:], rope_m2[:], MSK[:, 0:512], MSK[:, 512:1024]]
            for p in range(NCH):
                def _sqr(s, toks, p=p, gt=(go_tok if p == 0 else None),
                         gq=(("GQ", S("GQ")[1] - 3) if p >= 4 else None)):
                    if gt is not None:
                        wt(s, gt)
                    if gq is not None:
                        wt(s, gq)
                    i1 = s.activation(sqslot[p % 4],
                                      GO[:, p * ST:(p + 1) * ST], Act.Square)
                    rinc(i1, toks[0])
                [sq_t] = s_defer(_sqr, "SQB")

                @block.tensor
                def _(te):
                    wt(te, sq_t)
                    i1 = te.matmul(pb[5][0:1, 0:512], onescol[:],
                                   sqslot[p % 4],
                                   start=(p == 0), stop=(p == NCH - 1))
                    tokq = inc(i1, "GQ")
                    if p == NCH - 1:
                        bw[5] = tokq
            s_flush()

            @block.vector
            def _(v):
                brb(v, 4)
                brb(v, 5)
                tok = inc(v.tensor_copy(zrow[0:1, :], pb[4][0:1, 0:512]), "SV")
                wt(v, tok)
                tok = inc(v.tensor_tensor(zrow[0:1, :], zrow[0:1, :], zrow[0:1, :],
                                          Alu.mult), "SV")
                wt(v, tok)
                i1 = v.tensor_tensor(zrow[0:1, :], pb[5][0:1, 0:512],
                                     zrow[0:1, :], Alu.subtract)
                inc(i1, "SV")
            var_tok = now("SV")

            @block.scalar
            def _(s):
                wt(s, var_tok)
                i1 = s.activation(zrow[0:1, :], zrow[0:1, :],
                                  Act.Ln, bias=eps_sb[0:1, 0:1])
                tok = inc(i1, "SA")
                wt(s, tok)
                i1 = s.activation(rstdb[0:1, :], zrow[0:1, :],
                                  Act.Exp, scale=-0.5)
                inc(i1, "SA")
            srt_tok = now("SA")

            @block.vector
            def _(v):
                wt(v, srt_tok)
                i1 = v.scalar_tensor_tensor(r1[0:1, :], pb[4][0:1, 0:512], -1.0,
                                            rstdb[0:1, :], Alu.mult, Alu.mult)
                r1_tok2 = bre2(i1, 4, 5, "v")
            r1_tok = now("SV")

            @block.tensor
            def _(te):
                wt(te, r1_tok)
                bwb(te, 4)
                i1 = te.matmul(pb[4][:, 0:512], ones1[:], rstdb[:],
                               start=True, stop=True)
                bwe(i1, 4, "t")

            @block.vector
            def _(v):
                brb(v, 4)
                wt(v, st["sw"]["lnw"])
                wt(v, ktm_tok[NT - 1])
                for p in range(NCH):
                    i1 = v.scalar_tensor_tensor(ACORE[:, p * ST:(p + 1) * ST],
                                                GO[:, p * ST:(p + 1) * ST],
                                                lnw_f[:, p:p + 1], pb[4][:, 0:512],
                                                Alu.mult, Alu.mult)
                bre(i1, 4, "v")
            acore_tok = now("SV")
            st["ut_free"] = acore_tok

            res3_tok = [None] * NT
            ln4bn_tok = [None] * NT
            for t in range(NT):
                bA, bB = (0, 1) if t % 2 == 0 else (2, 3)

                @block.tensor
                def _(te):
                    if t == 0:
                        wt(te, acore_tok)
                        wt(te, st["c12_ready"])
                    bwb(te, bA)
                    bwb(te, bB)
                    for kc in range(NCH):
                        lhs = ACORE[:, kc * ST + 128 * t: kc * ST + 128 * t + 128]
                        te.matmul(pb[bA][:, 0:512], lhs, wch(W["wo"], kc, 0, 512),
                                  start=(kc == 0), stop=False)
                        te.matmul(pb[bB][:, 0:512], lhs, wch(W["wo"], kc, 512, 512),
                                  start=(kc == 0), stop=False)
                    rl = r1[0:1, 128 * t:128 * t + 128]
                    on = ones1[0:1, 0:128]
                    te.matmul(pb[bA][:, 0:512], rl, c1r[0:1, 0:512],
                              start=False, stop=False)
                    te.matmul(pb[bA][:, 0:512], on, c2r[0:1, 0:512],
                              start=False, stop=True)
                    te.matmul(pb[bB][:, 0:512], rl, c1r[0:1, 512:1024],
                              start=False, stop=False)
                    i1 = te.matmul(pb[bB][:, 0:512], on, c2r[0:1, 512:1024],
                                   start=False, stop=True)
                    bwe2(i1, bA, bB, "t")

                ln3_tok = ln_banks_apply(L3, t, bA, bB)

                @block.vector
                def _(v):
                    wt(v, ln3_tok)
                    v.tensor_tensor(slab[:, 1024 * t:1024 * t + 512],
                                    slab[:, 1024 * t:1024 * t + 512],
                                    pb[bA][:, 0:512], Alu.add)
                    i1 = v.tensor_tensor(slab[:, 1024 * t + 512:1024 * t + 1024],
                                         slab[:, 1024 * t + 512:1024 * t + 1024],
                                         pb[bB][:, 0:512], Alu.add)
                    bre2(i1, bA, bB, "v")
                    res3_tok[t] = now("SV")
                    wt(v, res3_tok[t])
                    v.bn_stats(L4["bn"][:, 12 * t:12 * t + 6],
                               slab[:, 1024 * t:1024 * t + 512])
                    i1 = v.bn_stats(L4["bn"][:, 12 * t + 6:12 * t + 12],
                                    slab[:, 1024 * t + 512:1024 * t + 1024])
                    ln4bn_tok[t] = inc(i1, "SV")

            s_flush()

            if LEVEL <= 2:
                early_store(res3_tok[NT - 1])
                return

            @block.vector
            def _(v):
                for t in range(NT):
                    wt(v, ln4bn_tok[t])
                    last = v.bn_aggr(L4["mv"][:, 2 * t:2 * t + 2],
                                     L4["bn"][:, 12 * t:12 * t + 12])
                inc(last, "SV")
            l4stats_tok = now("SV")

            @block.scalar
            def _(s):
                wt(s, l4stats_tok)
                i1 = s.activation(L4["sq"][:],
                                  bass.AP(L4["mv"], 1, [[2 * NT, 128], [2, NT]]),
                                  Act.Ln, bias=eps_sb[:, 0:1])
                tok = inc(i1, "SA")
                wt(s, tok)
                i1 = s.activation(L4["rs"][:], L4["sq"][:], Act.Exp, scale=-0.5)
                inc(i1, "SA")
            l4sq_tok = now("SA")

            @block.vector
            def _(v):
                wt(v, l4sq_tok)
                i1 = v.scalar_tensor_tensor(
                    L4["nm"][:], bass.AP(L4["mv"], 0, [[2 * NT, 128], [2, NT]]),
                    -1.0, L4["rs"][:], Alu.mult, Alu.mult)
                inc(i1, "SV")
            ln4_tok = now("SV")
            t4_tok = [None] * NT

            @block.scalar
            def _(s):
                wt(s, ln4_tok)
                for t in range(NT):
                    i1 = s.activation(TS[:, 1024 * t:1024 * t + 1024],
                                      slab[:, 1024 * t:1024 * t + 1024], Act.Identity,
                                      bias=L4["nm"][:, t:t + 1],
                                      scale=L4["rs"][:, t:t + 1])
                    t4_tok[t] = inc(i1, "SA")

            tt_evac = [None] * NT
            for t in range(NT):
                @block.tensor
                def _(te):
                    wt(te, t4_tok[t])
                    for half in range(2):
                        b = 6 + half
                        bwb(te, b)
                        for ch4 in range(4):
                            ch = 4 * half + ch4
                            i1 = te.transpose(
                                TB[half][:, 128 * ch4:128 * ch4 + 128],
                                TS[:, 1024 * t + 128 * ch: 1024 * t + 128 * ch + 128],
                                ident[:])
                        bwe(i1, b, "t")

                @block.vector
                def _(v):
                    for half in range(2):
                        b = 6 + half
                        brb(v, b)
                        i1 = v.tensor_copy(
                            bass.AP(TTs, (4 * half) * ST + 128 * t,
                                    [[NCH * ST, 128], [ST, 4], [1, 128]]),
                            TB[half][:, 0:512])
                        bre(i1, b, "v")
                tt_evac[t] = now("SV")
            tt_tok = tt_evac[NT - 1]

            @block.tensor
            def _(te):
                wt(te, tt_tok)
                wt(te, st["sw"]["fa"])
                bwb(te, 4)
                for kc in range(NCH):
                    i1 = te.matmul(pb[4][0:16, 0:512], fa_sb[:, kc * 16:(kc + 1) * 16],
                                   TTs[:, kc * ST:(kc + 1) * ST],
                                   start=(kc == 0), stop=(kc == NCH - 1))
                bwe(i1, 4, "t")

            @block.vector
            def _(v):
                brb(v, 4)
                i1 = v.tensor_copy(g1_sb[0:16, :], pb[4][0:16, 0:512])
                bre(i1, 4, "v")
            l1_tok = now("SV")

            if LEVEL == 3 and g + 1 < NSUP:
                emit_head_front(g + 1)
                emit_head_back(g + 1)

            res5_tok = [None] * NT
            for t in range(NT):
                bA, bB = (0, 1) if t % 2 == 0 else (2, 3)

                @block.tensor
                def _(te):
                    if t == 0:
                        wt(te, st["wf_ready"])
                        wt(te, st["sw"]["fbias"])
                        wt(te, st["sw"]["fb"])
                        wt(te, l1_tok)
                    bwb(te, bA)
                    bwb(te, bB)
                    for kc in range(NCH):
                        lhs = TTs[:, kc * ST + 128 * t: kc * ST + 128 * t + 128]
                        te.matmul(pb[bA][:, 0:512], lhs, wch(W["wf"], kc, 0, 512),
                                  start=(kc == 0), stop=False)
                        te.matmul(pb[bB][:, 0:512], lhs, wch(W["wf"], kc, 512, 512),
                                  start=(kc == 0), stop=False)
                    te.matmul(pb[bA][:, 0:512], ones1[:], fbias_sb[0:1, 0:512],
                              start=False, stop=False)
                    te.matmul(pb[bB][:, 0:512], ones1[:], fbias_sb[0:1, 512:1024],
                              start=False, stop=False)
                    ll = g1_sb[0:16, 128 * t:128 * t + 128]
                    te.matmul(pb[bA][:, 0:512], ll, fb_sb[:, 0:512],
                              start=False, stop=True)
                    i1 = te.matmul(pb[bB][:, 0:512], ll, fb_sb[:, 512:1024],
                                   start=False, stop=True)
                    bwe2(i1, bA, bB, "t")

                ln5_tok = ln_banks_apply(L5, t, bA, bB)

                @block.vector
                def _(v):
                    wt(v, ln5_tok)
                    v.tensor_tensor(slab[:, 1024 * t:1024 * t + 512],
                                    slab[:, 1024 * t:1024 * t + 512],
                                    pb[bA][:, 0:512], Alu.add)
                    i1 = v.tensor_tensor(slab[:, 1024 * t + 512:1024 * t + 1024],
                                         slab[:, 1024 * t + 512:1024 * t + 1024],
                                         pb[bB][:, 0:512], Alu.add)
                    bre2(i1, bA, bB, "v")
                    res5_tok[t] = now("SV")

            s_flush()
            if LEVEL == 3 and g + 1 < NSUP:
                emit_head_trig(g + 1)

            sn_store = "SO%d" % (g % 2)

            @block.gpsimd
            def _(sy):
                for t in range(NT):
                    wt(sy, res5_tok[t])
                row0 = (g - n_pre) * ST
                sy.dma_start(
                    bass.AP(out_h, row0 * D, [[D, 128], [128 * D, NT], [1, D]]),
                    bass.AP(slab, 0, [[NT * D, 128], [D, NT], [1, D]]),
                ).then_inc(S(sn_store)[0], 16)
                S(sn_store)[1] += 16
            st["slab_free"][g % 2] = (sn_store, S(sn_store)[1])

            st["kt_free"] = now("SP")
            st["qt_free"] = now("SP")
            st["vs_free"] = tt_tok

            if g < 4 and LEVEL != -4 and not (NO_QUANT or NO_QP2):
                emit_qp2_loads(8)
            if LEVEL != 3 and LEVEL != -4 and not (NO_QUANT or NO_QP2):
                emit_qp2_ladder(8)

        emit_head(0)
        for g in range(NSUP):
            if g == n_pre and LEVEL != -4 and not (NO_QUANT or NO_QP2):
                emit_qp2_loads(32)
                emit_qp2_ladder(32)
            emit_body(g)
            if LEVEL != 3 and g + 1 < NSUP:
                emit_head(g + 1)

        @block.gpsimd
        def _(sy):
            sy.wait_ge(S("SO0")[0], S("SO0")[1])
            sy.wait_ge(S("SO1")[0], S("SO1")[1])

    return nc


def _trig_tables(pos):
    import jax
    import jax.numpy as jnp
    d = 64
    with jax.default_device(jax.devices("cpu")[0]):
        j32 = jnp.arange(0, d, 2, dtype=jnp.float32)
        inv = 1.0 / 10000.0 ** (j32 / d)
        pos32 = jnp.asarray(np.asarray(pos), dtype=jnp.float32)
        ang = pos32[:, None] * inv[None, :]
        s32 = np.asarray(jnp.sin(ang))
        c32 = np.asarray(jnp.cos(ang))
    su = s32.astype(np.float64)
    cu = c32.astype(np.float64)
    r_hi = np.arcsin(np.clip(su, -1.0, 1.0)).astype(np.float32)
    u_hi = np.arcsin(np.clip(cu, -1.0, 1.0)).astype(np.float32)
    s0 = np.sin(r_hi.astype(np.float64))
    c1 = np.sin(u_hi.astype(np.float64))
    with np.errstate(divide="ignore", invalid="ignore"):
        r_lo = np.where(c1 != 0.0, (su - s0) / c1, 0.0).astype(np.float32)
        w = np.where(s0 != 0.0, (c1 - cu) / s0, 0.0).astype(np.float32)
    u_lo = None

    sig = np.ones(64, np.float32)
    sig[:32] = -1.0
    T = len(pos)

    def fm(tab, signed):
        full = np.empty((128, T), np.float32)
        for p in range(128):
            j = p % 64
            v = tab[:, j % 32]
            if signed:
                v = v * sig[j]
            full[p] = v
        return full

    return {"trh": fm(r_hi, True), "trl": fm(r_lo, True),
            "tuh": fm(u_hi, False), "tw": fm(w, True)}


_PROG_CACHE = {}


def _get_prog(n_pre, n_own):
    key = (n_pre, n_own)
    if key not in _PROG_CACHE:
        _PROG_CACHE[key] = build_program(n_pre, n_own)
    return _PROG_CACHE[key]


def make_in_maps(inputs, n_pre, n_own, n_batches):
    x = np.ascontiguousarray(np.asarray(inputs["L0_attn_x"], np.float32))
    pre_tok, own_tok = n_pre * ST, n_own * ST
    shared = {
        "wq": np.ascontiguousarray(inputs["L0_attn_Wq"], np.float32),
        "wk": np.ascontiguousarray(inputs["L0_attn_Wk"], np.float32),
        "wv": np.ascontiguousarray(inputs["L0_attn_Wv"], np.float32),
        "wo": np.ascontiguousarray(inputs["L0_attn_Wo"], np.float32),
        "wg1": np.ascontiguousarray(inputs["L0_attn_Wg1"], np.float32),
        "wg2": np.ascontiguousarray(inputs["L0_attn_Wg2"], np.float32),
        "lnw": np.ascontiguousarray(
            np.asarray(inputs["L0_attn_ln_w"], np.float32).reshape(NCH, 128).T),
        "lnb": np.ascontiguousarray(
            np.asarray(inputs["L0_attn_ln_b"], np.float32).reshape(NCH, 128).T),
        "wf": np.ascontiguousarray(inputs["L0_ffn_kernel"], np.float32),
        "fbias": np.ascontiguousarray(inputs["L0_ffn_bias"], np.float32),
        "fa": np.ascontiguousarray(inputs["L0_ffn_a"], np.float32),
        "fb": np.ascontiguousarray(inputs["L0_ffn_b"], np.float32),
    }
    kq = np.arange(128)
    shared["maskt"] = (kq[:, None] <= kq[None, :]).astype(np.float32)
    p128_m = np.zeros((128, 128), np.float32)
    for p in range(128):
        blk, j = p // 64 * 64, p % 64
        p128_m[p, blk + (j + 32) % 64] = 1.0
    shared["p128"] = p128_m
    shared["ident"] = np.eye(128, dtype=np.float32)

    trig_pre = _trig_tables(np.arange(pre_tok, dtype=np.float64)) if pre_tok else None
    in_maps = []
    for b in range(n_batches):
        for half in range(2):
            own0 = half * own_tok
            tr_own = _trig_tables(own0 + np.arange(own_tok, dtype=np.float64))
            m = dict(shared)
            m["x_own"] = np.ascontiguousarray(x[b, own0:own0 + own_tok])
            if half == 1 and pre_tok:
                m["x_pre"] = np.ascontiguousarray(x[b, 0:pre_tok])
            else:
                m["x_pre"] = np.zeros((max(pre_tok, ST), D), np.float32)
            for n in ("trh", "trl", "tuh", "tw"):
                if pre_tok:
                    m[n] = np.ascontiguousarray(
                        np.concatenate([trig_pre[n], tr_own[n]], axis=1))
                else:
                    m[n] = np.ascontiguousarray(tr_own[n])
            in_maps.append(m)
    return in_maps


def kernel(**inputs):
    n_pre, n_own = 4, 4
    x = np.asarray(inputs["L0_attn_x"])
    B, N, _ = x.shape
    in_maps = make_in_maps(inputs, n_pre, n_own, B)
    nc = _get_prog(n_pre, n_own)
    res = run_bass_kernel_spmd(nc, in_maps, core_ids=list(range(len(in_maps))))
    out = np.empty((B, N, D), np.float32)
    own_tok = n_own * ST
    for b in range(B):
        for half in range(2):
            shard = np.asarray(res.results[2 * b + half]["out"])
            out[b, half * own_tok:(half + 1) * own_tok] = shard
    return out


# version 21
# speedup vs baseline: 1.0214x; 1.0214x over previous
import numpy as np
import concourse.bass as bass
import concourse.mybir as mybir
from concourse.bass_utils import run_bass_kernel_spmd

F32 = mybir.dt.float32
BF16 = mybir.dt.bfloat16
Alu = mybir.AluOpType
Act = mybir.ActivationFunctionType
AX = mybir.AxisListType.X

D = 1024
LEVEL = 3
NO_QUANT = False
NO_QP2 = False
NCH = 8
HD = 64
ST = 512
NT = 4
EPS = 1e-5


def build_program(n_pre, n_own):
    nc = bass.Bass("TRN2", target_bir_lowering=False)
    NOWNTOK = n_own * ST
    NPRETOK = max(n_pre, 1) * ST
    NSUP = n_pre + n_own
    NCOL = NSUP * ST

    sems = {}

    def S(name):
        if name not in sems:
            sems[name] = [nc.alloc_semaphore(name), 0]
        return sems[name]

    def inc(instr, name, k=1):
        s = S(name)
        instr.then_inc(s[0], k)
        s[1] += k
        return (name, s[1])

    def wt(eng, tok):
        if tok is not None:
            eng.wait_ge(S(tok[0])[0], tok[1])

    def now(name):
        return (name, S(name)[1])

    x_own = nc.dram_tensor("x_own", [NOWNTOK, D], F32, kind="ExternalInput")
    x_pre = nc.dram_tensor("x_pre", [NPRETOK, D], F32, kind="ExternalInput")
    w_h = {n: nc.dram_tensor(n, [D, D], F32, kind="ExternalInput")
           for n in ("wq", "wk", "wv", "wo", "wf")}
    wg1_h = nc.dram_tensor("wg1", [D, HD], F32, kind="ExternalInput")
    wg2_h = nc.dram_tensor("wg2", [HD, D], F32, kind="ExternalInput")
    lnw_h = nc.dram_tensor("lnw", [128, NCH], F32, kind="ExternalInput")
    lnb_h = nc.dram_tensor("lnb", [128, NCH], F32, kind="ExternalInput")
    fbias_h = nc.dram_tensor("fbias", [D], F32, kind="ExternalInput")
    fa_h = nc.dram_tensor("fa", [D, 16], F32, kind="ExternalInput")
    fb_h = nc.dram_tensor("fb", [16, D], F32, kind="ExternalInput")
    trig_h = {n: nc.dram_tensor(n, [128, NCOL], F32, kind="ExternalInput")
              for n in ("trh", "trl", "tuh", "tw")}
    maskt_h = nc.dram_tensor("maskt", [128, 128], F32, kind="ExternalInput")
    p128_h = nc.dram_tensor("p128", [128, 128], F32, kind="ExternalInput")
    ident_h = nc.dram_tensor("ident", [128, 128], F32, kind="ExternalInput")
    out_h = nc.dram_tensor("out", [NOWNTOK, D], F32, kind="ExternalOutput")
    scr_h = nc.dram_tensor("scr", [1, 132], F32)

    A = nc.alloc_sbuf_tensor
    W = {n: A(n + "_sb", [128, NCH * D], BF16) for n in ("wq", "wk", "wv", "wo", "wf")}
    wg1_sb = A("wg1_sb", [128, NCH * HD], BF16)
    wg2_sb = A("wg2_sb", [64, D], BF16)
    fa_sb = A("fa_sb", [128, NCH * 16], BF16)
    fb_sb = A("fb_sb", [16, D], BF16)
    fbias_sb = A("fbias_sb", [1, D], BF16)
    lnw_f = A("lnw_f", [128, NCH], F32)
    lnwb = A("lnwb", [128, NCH], BF16)
    lnbb = A("lnbb", [128, NCH], BF16)
    c1r = A("c1row", [1, D], BF16)
    c2r = A("c2row", [1, D], BF16)
    ones1 = A("ones1", [1, 128], BF16)
    onescol = A("onescol", [128, 1], BF16)
    onesf = A("onesf", [1, 128], F32)
    eps_sb = A("eps_sb", [128, 1], F32)
    r1 = A("r1", [1, ST], BF16)
    rstdb = A("rstdb", [1, ST], BF16)
    maskt = A("maskt_s", [128, 128], BF16)
    p128 = A("p128_s", [128, 128], BF16)
    ident = A("ident_s", [128, 128], BF16)

    XS = [A("XAslab", [128, NT * D], F32), A("XBslab", [128, NT * D], F32)]
    UT = A("UTslab", [128, NCH * ST], BF16, is_transpose_dest=True)
    QTA = A("QTAslab", [128, NCH * ST], BF16)
    QTB = A("QTBslab", [128, NCH * ST], BF16)
    KT = A("KTslab", [128, NCH * ST], BF16, is_transpose_dest=True)
    VS = A("VSslab", [128, NT * D], BF16)
    GT = A("GTslab", [128, NCH * ST], BF16)
    GO, ACORE, TTs, TS = UT, KT, KT, VS
    KTM_r = A("KTM_ring", [128, 2 * D], BF16, is_transpose_dest=True)
    MSK = A("MSK", [128, 1024], BF16)
    S_acc = A("S_acc", [128, 512], F32)
    S_sb = A("S_sb", [128, 1024], BF16)
    rope_m1 = A("rope_m1", [128, ST], BF16)
    rope_m2 = A("rope_m2", [128, ST], BF16)

    sin_r = A("sin_ring", [128, ST], BF16)
    cos_r = A("cos_ring", [128, ST], BF16)
    tqa = A("tqa", [128, 256], F32)
    tqb = A("tqb", [128, 256], F32)
    tqc = A("tqc", [128, 256], F32)
    tqd = A("tqd", [128, 256], F32)
    s0_sb = A("s0_sb", [128, ST], BF16)
    c1_sb = A("c1_sb", [128, ST], BF16)

    qsc = [A("qsc0", [128, 256], F32), A("qsc1", [128, 256], F32)]
    qsg = [A("qsg0", [128, 256], F32), A("qsg1", [128, 256], F32)]
    qu = [A("qu0", [128, 256], F32), A("qu1", [128, 256], F32)]
    qt_ = A("qt", [128, 256], F32)
    qr = A("qr", [128, 256], F32)
    mx8 = A("mx8", [128, 20], F32)
    m1r = A("m1r", [1, 128], F32)
    mrow = A("mrow", [1, 2], F32)
    scl128 = A("scl128", [128, 2], F32)

    def ln_bufs(tag):
        return {"bn": A(tag + "_bn", [128, 12 * NT], F32),
                "mv": A(tag + "_mv", [128, 2 * NT], F32),
                "sq": A(tag + "_sq", [128, NT], F32),
                "rs": A(tag + "_rs", [128, NT], F32),
                "nm": A(tag + "_nm", [128, NT], F32)}
    L1, L3, L4, L5 = (ln_bufs(t) for t in ("l1", "l3", "l4", "l5"))
    zrow = A("zrow", [1, ST], F32)
    g1_sb = A("g1_sb", [64, ST], BF16)

    pb = [nc.alloc_psum_tensor(f"pb{i}", [128, 512], F32) for i in range(6)]
    TB = [nc.alloc_psum_tensor(f"tb{i}", [128, 1024], BF16) for i in range(2)]
    bw = [None] * 8
    br = [[] for _ in range(8)]
    ESEM = {"t": "SP", "v": "SV", "s": "SA", "g": "SG"}

    def bwb(eng, b):
        for tok in br[b]:
            wt(eng, tok)
        br[b].clear()

    def bwe(instr, b, e):
        bw[b] = inc(instr, ESEM[e])
        return bw[b]

    def brb(eng, b):
        if bw[b] is not None:
            wt(eng, bw[b])

    def bre(instr, b, e):
        tok = inc(instr, ESEM[e])
        br[b].append(tok)
        return tok

    def bwe2(instr, b1, b2, e):
        tok = inc(instr, ESEM[e])
        bw[b1] = tok
        bw[b2] = tok
        return tok

    def bre2(instr, b1, b2, e):
        tok = inc(instr, ESEM[e])
        br[b1].append(tok)
        br[b2].append(tok)
        return tok

    def wch(wsb, kc, o0, n):
        return wsb[:, kc * D + o0: kc * D + o0 + n]

    sdefer = []

    def s_defer(fn, esem, n=1):
        s = S(esem)
        toks = [(esem, s[1] + i + 1) for i in range(n)]
        s[1] += n
        sdefer.append((fn, toks))
        return toks

    def rinc(instr, tok, k=1):
        instr.then_inc(S(tok[0])[0], k)

    st = {"xload": {}, "slab_free": [None, None], "u_slot": None,
          "ktm_slot": [None, None], "h": {}, "sb_cur": 0, "s_war": None,
          "trig_dma": {},
          "sin_slot": None, "trig_scr": None, "s_chain": None,
          "wf_ready": None, "c12_ready": None, "sw": {}, "gt_free": None,
          "ut_free": None, "kt_free": None, "qt_free": None, "vs_free": None,
          "rope_last": None}

    with nc.Block() as block:
        def s_flush():
            pend = list(sdefer)
            del sdefer[:]
            for fn_, toks_ in pend:
                def mk(fn=fn_, toks=toks_):
                    @block.scalar
                    def _(s):
                        fn(s, toks)
                mk()

        @block.gpsimd
        def _(g):
            def ld(name, dst, src):
                sname = "SW_" + name
                g.dma_start(dst, src).then_inc(S(sname)[0], 16)
                S(sname)[1] += 16
                st["sw"][name] = (sname, S(sname)[1])

            def ld4(name, sb, h):
                sname = "SW_" + name
                for i in range(4):
                    g.dma_start(
                        bass.AP(sb, i * 2 * D, [[NCH * D, 128], [D, 2], [1, D]]),
                        bass.AP(h, i * 2 * 128 * D, [[D, 128], [128 * D, 2], [1, D]]),
                    ).then_inc(S(sname)[0], 16)
                    S(sname)[1] += 16
                st["sw"][name] = (sname, S(sname)[1])

            ld("p128", p128[:], p128_h.ap())
            ld("ident", ident[:], ident_h.ap())
            ld4("wk", W["wk"], w_h["wk"])
            ld4("wv", W["wv"], w_h["wv"])
            ld4("wq", W["wq"], w_h["wq"])
            ld("wg1", bass.AP(wg1_sb, 0, [[NCH * HD, 128], [HD, NCH], [1, HD]]),
               bass.AP(wg1_h, 0, [[HD, 128], [128 * HD, NCH], [1, HD]]))
            ld("wg2", wg2_sb[:], wg2_h.ap())
            ld("maskt", maskt[:], maskt_h.ap())
            ld4("wo", W["wo"], w_h["wo"])
            ld("lnw", lnw_f[:], lnw_h.ap())
            ld("lnwb", lnwb[:], lnw_h.ap())
            ld("lnbb", lnbb[:], lnb_h.ap())
            ld("fa", bass.AP(fa_sb, 0, [[NCH * 16, 128], [16, NCH], [1, 16]]),
               bass.AP(fa_h, 0, [[16, 128], [128 * 16, NCH], [1, 16]]))
            ld("fb", fb_sb[:], fb_h.ap())
            ld("fbias", fbias_sb[:], bass.AP(fbias_h, 0, [[D, 1], [1, D]]))
            g.memset(ones1[:], 1.0)
            g.memset(onescol[:], 1.0 / 1024.0)
            g.memset(onesf[:], 1.0)
            g.memset(eps_sb[:], EPS)
            g.memset(S_acc[:], 0.0)
            g.memset(QTA[64:128, :], 0.0)
            g.memset(QTB[0:64, :], 0.0)
            st["init"] = inc(g.memset(S_sb[:], 0.0), "SG")
            st["s_chain"] = st["init"]

        def emit_xload(gi):
            slab = XS[gi % 2]
            src = x_pre if gi < n_pre else x_own
            row0 = (gi if gi < n_pre else gi - n_pre) * ST

            sname = "SXs%d" % (gi % 2)

            @block.sync
            def _(sy):
                wt(sy, st["slab_free"][gi % 2])
                sy.dma_start(
                    bass.AP(slab, 0, [[NT * D, 128], [D, NT], [1, D]]),
                    bass.AP(src, row0 * D, [[D, 128], [128 * D, NT], [1, D]]),
                ).then_inc(S(sname)[0], 16)
                S(sname)[1] += 16
            st["xload"][gi] = (sname, S(sname)[1])

        emit_xload(0)

        @block.sync
        def _(sy):
            if NO_QUANT:
                return
            sy.dma_start(XS[1][:],
                         bass.AP(w_h["wf"], 0, [[4096, 128], [1, 4096]])
                         ).then_inc(S("SXP")[0], 16)
            S("SXP")[1] += 16

        @block.vector
        def _(v):
            if NO_QUANT:
                return
            v.wait_ge(S("SXP")[0], 16)
            for c in range(8):
                i1 = v.tensor_reduce(mx8[:, c:c + 1],
                                     XS[1][:, 512 * c:512 * c + 512], AX, Alu.max,
                                     apply_absolute_value=True)
                inc(i1, "QR")

        @block.sync
        def _(sy):
            if NO_QUANT:
                return
            sy.wait_ge(S("QR")[0], 8)
            sy.dma_start(XS[1][:],
                         bass.AP(w_h["wf"], 4096 * 128, [[4096, 128], [1, 4096]])
                         ).then_inc(S("SXP")[0], 16)
            S("SXP")[1] += 16

        @block.vector
        def _(v):
            if NO_QUANT:
                return
            v.wait_ge(S("SXP")[0], 32)
            for c in range(8):
                i1 = v.tensor_reduce(mx8[:, 8 + c:9 + c],
                                     XS[1][:, 512 * c:512 * c + 512], AX, Alu.max,
                                     apply_absolute_value=True)
                inc(i1, "QR")
            wt(v, now("QR"))
            inc(v.tensor_reduce(mx8[:, 16:17], mx8[:, 0:16], AX, Alu.max), "SV")
        red_tok = now("SV")
        if not NO_QUANT:
            st["slab_free"][1] = red_tok

        @block.gpsimd
        def _(g):
            if NO_QUANT:
                return
            wt(g, red_tok)
            g.dma_start(bass.AP(scr_h, 2, [[1, 1], [1, 128]]),
                        bass.AP(mx8, 16, [[20, 128], [1, 1]])).then_inc(S("SWB")[0], 16)
            S("SWB")[1] += 16
            g.wait_ge(S("SWB")[0], S("SWB")[1])
            g.dma_start(m1r[:], bass.AP(scr_h, 2, [[1, 1], [1, 128]])
                        ).then_inc(S("SWB")[0], 16)
            S("SWB")[1] += 16
        bounce_tok = ("SWB", S("SWB")[1])

        @block.vector
        def _(v):
            if NO_QUANT:
                return
            wt(v, bounce_tok)
            t = inc(v.tensor_reduce(mrow[0:1, 0:1], m1r[:], AX, Alu.max), "SV")
            wt(v, t)
            t = inc(v.reciprocal(mrow[0:1, 1:2], mrow[0:1, 0:1]), "SV")
            wt(v, t)
            t = inc(v.tensor_scalar(mrow[0:1, 0:1], mrow[0:1, 0:1],
                                    0.14285714285714285, None, Alu.mult), "SV")
            wt(v, t)
            inc(v.tensor_scalar(mrow[0:1, 1:2], mrow[0:1, 1:2], 7.0, None,
                                Alu.mult), "SV")
        mrow_tok = now("SV")

        @block.tensor
        def _(t):
            if NO_QUANT:
                return
            wt(t, mrow_tok)
            wt(t, st["init"])
            bwb(t, 0)
            i1 = t.matmul(pb[0][:, 0:2], onesf[:], mrow[:], start=True, stop=True)
            bwe(i1, 0, "t")

        @block.vector
        def _(v):
            if NO_QUANT:
                return
            brb(v, 0)
            i1 = v.tensor_copy(scl128[:], pb[0][:, 0:2])
            bre(i1, 0, "v")
        scl_tok = now("SV")

        qp2 = {"loads": 0, "done": 0}

        def emit_qp2_loads(n):
            @block.gpsimd
            def _(sy):
                for _i in range(n):
                    i = qp2["loads"]
                    if i >= 32:
                        return
                    kc, qr_ = i // 4, i % 4
                    if i >= 2:
                        sy.wait_ge(S("QA")[0], 2 * (i - 1))
                    sq = "SXQ2%d" % (i % 2)
                    sy.dma_start(qsc[i % 2][:],
                                 bass.AP(w_h["wf"], kc * 128 * D + qr_ * 256,
                                         [[D, 128], [1, 256]])
                                 ).then_inc(S(sq)[0], 16)
                    S(sq)[1] += 16
                    qp2["loads"] += 1

        def emit_qp2_ladder(n):
            for _i in range(n):
                i = qp2["done"]
                if i >= 32:
                    return
                kc, qr_ = i // 4, i % 4
                w = qsc[i % 2]

                def _lad(s, toks, i=i, w=w):
                    s.wait_ge(S("SXQ2%d" % (i % 2))[0], 16 * (i // 2 + 1))
                    if i == 0:
                        wt(s, scl_tok)
                    if i >= 2:
                        s.wait_ge(S("QD")[0], i - 1)
                    s.activation(qu[i % 2][:], w[:], Act.Abs, scale=scl128[:, 1:2])
                    i1 = s.activation(qsg[i % 2][:], w[:], Act.Sign)
                    rinc(i1, toks[0], 2)
                s_defer(_lad, "QA", 2)

                @block.vector
                def _(g):
                    def gs(i1):
                        tok = inc(i1, "SV")
                        wt(g, tok)
                    g.wait_ge(S("QA")[0], 2 * (i + 1))
                    gs(g.tensor_scalar(qt_[:], qu[i % 2][:], 3.5, None, Alu.is_ge))
                    gs(g.scalar_tensor_tensor(qr[:], qt_[:], -4.0, qu[i % 2][:],
                                              Alu.mult, Alu.add))
                    gs(g.tensor_scalar(qt_[:], qr[:], 1.5, None, Alu.is_ge))
                    gs(g.scalar_tensor_tensor(qr[:], qt_[:], -2.0, qr[:],
                                              Alu.mult, Alu.add))
                    gs(g.tensor_scalar(qt_[:], qr[:], 0.5, None, Alu.is_ge))
                    gs(g.scalar_tensor_tensor(qr[:], qt_[:], -1.0, qr[:],
                                              Alu.mult, Alu.add))
                    gs(g.tensor_tensor(qr[:], qu[i % 2][:], qr[:], Alu.subtract))
                    gs(g.tensor_tensor(qr[:], qr[:], qsg[i % 2][:], Alu.mult))
                    i1 = g.tensor_scalar(
                        W["wf"][:, kc * D + qr_ * 256: kc * D + qr_ * 256 + 256],
                        qr[:], scl128[:, 0:1], None, Alu.mult)
                    inc(i1, "QD")
                qp2["done"] += 1
                if qp2["done"] == 32:
                    st["wf_ready"] = ("QD", 32)
            s_flush()

        def emit_c12():
            for row in range(4):
                vec = lnwb if row < 2 else lnbb
                dst = c1r if row < 2 else c2r
                o0 = 512 * (row % 2)

                cb = 4 + row % 2

                @block.tensor
                def _(t):
                    if row == 0:
                        wt(t, st["sw"]["wo"])
                        wt(t, st["sw"]["lnwb"])
                        wt(t, st["sw"]["lnbb"])
                    bwb(t, cb)
                    last = None
                    for kc in range(NCH):
                        last = t.matmul(pb[cb][0:1, 0:512],
                                        vec[:, kc:kc + 1], wch(W["wo"], kc, o0, 512),
                                        start=(kc == 0), stop=(kc == NCH - 1))
                    bwe(last, cb, "t")

                @block.vector
                def _(v):
                    brb(v, cb)
                    i1 = v.tensor_copy(dst[0:1, o0:o0 + 512], pb[cb][0:1, 0:512])
                    bre(i1, cb, "v")
            st["c12_ready"] = now("SV")

        def ln_stats_slab(L, slab, pre_tok):
            @block.vector
            def _(v):
                wt(v, pre_tok)
                last = None
                for t in range(NT):
                    v.bn_stats(L["bn"][:, 12 * t:12 * t + 6],
                               slab[:, 1024 * t:1024 * t + 512])
                    last = v.bn_stats(L["bn"][:, 12 * t + 6:12 * t + 12],
                                      slab[:, 1024 * t + 512:1024 * t + 1024])
                tok = inc(last, "SV")
                wt(v, tok)
                for t in range(NT):
                    last = v.bn_aggr(L["mv"][:, 2 * t:2 * t + 2],
                                     L["bn"][:, 12 * t:12 * t + 12])
                inc(last, "SV")
            stats_tok = now("SV")

            @block.scalar
            def _(s):
                wt(s, stats_tok)
                i1 = s.activation(L["sq"][:],
                                  bass.AP(L["mv"], 1, [[2 * NT, 128], [2, NT]]),
                                  Act.Ln, bias=eps_sb[:, 0:1])
                tok = inc(i1, "SA")
                wt(s, tok)
                i1 = s.activation(L["rs"][:], L["sq"][:], Act.Exp, scale=-0.5)
                inc(i1, "SA")
            sq_tok = now("SA")

            @block.vector
            def _(v):
                wt(v, sq_tok)
                i1 = v.scalar_tensor_tensor(
                    L["nm"][:], bass.AP(L["mv"], 0, [[2 * NT, 128], [2, NT]]),
                    -1.0, L["rs"][:], Alu.mult, Alu.mult)
                inc(i1, "SV")
            return now("SV")

        def ln_stats_banks(L, t, bA, bB):
            @block.vector
            def _(v):
                brb(v, bA)
                brb(v, bB)
                v.bn_stats(L["bn"][:, 12 * t:12 * t + 6], pb[bA][:, 0:512])
                i1 = v.bn_stats(L["bn"][:, 12 * t + 6:12 * t + 12], pb[bB][:, 0:512])
                tok = bre2(i1, bA, bB, "v")
                wt(v, tok)
                i1 = v.bn_aggr(L["mv"][:, 2 * t:2 * t + 2],
                               L["bn"][:, 12 * t:12 * t + 12])
                inc(i1, "SV")
            stats_tok = now("SV")

            def _sq(s, toks, L=L, t=t, stats_tok=stats_tok):
                wt(s, stats_tok)
                i1 = s.activation(L["sq"][:, t:t + 1], L["mv"][:, 2 * t + 1:2 * t + 2],
                                  Act.Ln, bias=eps_sb[:, 0:1])
                rinc(i1, toks[0])
                wt(s, toks[0])
                i1 = s.activation(L["rs"][:, t:t + 1], L["sq"][:, t:t + 1],
                                  Act.Exp, scale=-0.5)
                rinc(i1, toks[1])
            sq_tok = s_defer(_sq, "SQB", 2)[1]

            @block.vector
            def _(v):
                wt(v, sq_tok)
                i1 = v.scalar_tensor_tensor(L["nm"][:, t:t + 1],
                                            L["mv"][:, 2 * t:2 * t + 1], -1.0,
                                            L["rs"][:, t:t + 1], Alu.mult, Alu.mult)
                inc(i1, "SV")
            return now("SV")

        def ln_banks_apply(L, t, bA, bB):
            @block.vector
            def _(v):
                brb(v, bA)
                brb(v, bB)
                v.bn_stats(L["bn"][:, 12 * t:12 * t + 6], pb[bA][:, 0:512])
                i1 = v.bn_stats(L["bn"][:, 12 * t + 6:12 * t + 12], pb[bB][:, 0:512])
                tok = bre2(i1, bA, bB, "v")
                wt(v, tok)
                i1 = v.bn_aggr(L["mv"][:, 2 * t:2 * t + 2],
                               L["bn"][:, 12 * t:12 * t + 12])
                inc(i1, "SV")
            stats_tok = now("SV")
            cell = [None]
            wA, wB = bw[bA], bw[bB]

            def _sqap(s, toks, L=L, t=t, bA=bA, bB=bB, stats_tok=stats_tok,
                      wA=wA, wB=wB):
                wt(s, stats_tok)
                i1 = s.activation(L["sq"][:, t:t + 1],
                                  L["mv"][:, 2 * t + 1:2 * t + 2],
                                  Act.Ln, bias=eps_sb[:, 0:1])
                rinc(i1, toks[0])
                wt(s, toks[0])
                i1 = s.activation(L["rs"][:, t:t + 1], L["sq"][:, t:t + 1],
                                  Act.Exp, scale=-0.5)
                rinc(i1, toks[1])
                wt(s, cell[0])
                wt(s, wA)
                wt(s, wB)
                s.activation(pb[bA][:, 0:512], pb[bA][:, 0:512], Act.Identity,
                             bias=L["nm"][:, t:t + 1], scale=L["rs"][:, t:t + 1])
                i1 = s.activation(pb[bB][:, 0:512], pb[bB][:, 0:512], Act.Identity,
                                  bias=L["nm"][:, t:t + 1],
                                  scale=L["rs"][:, t:t + 1])
                rinc(i1, toks[2])
            toks = s_defer(_sqap, "SQB", 3)
            sq_tok, ap_done = toks[1], toks[2]

            @block.vector
            def _(v):
                wt(v, sq_tok)
                i1 = v.scalar_tensor_tensor(L["nm"][:, t:t + 1],
                                            L["mv"][:, 2 * t:2 * t + 1], -1.0,
                                            L["rs"][:, t:t + 1], Alu.mult, Alu.mult)
                cell[0] = inc(i1, "SV")
            bw[bA] = ap_done
            bw[bB] = ap_done
            return ap_done

        def emit_trig_loads(g, hw, ab_tok=None):
            c0 = g * ST + 256 * hw

            @block.sync
            def _(sy):
                wt(sy, ab_tok if ab_tok is not None else st["trig_scr"])
                for n, buf in (("trh", tqa), ("tuh", tqb)):
                    sy.dma_start(buf[:],
                                 bass.AP(trig_h[n], c0, [[NCOL, 128], [1, 256]])
                                 ).then_inc(S("SXTA")[0], 16)
                    S("SXTA")[1] += 16
                wt(sy, st["trig_scr"])
                for n, buf in (("trl", tqc), ("tw", tqd)):
                    sy.dma_start(buf[:],
                                 bass.AP(trig_h[n], c0, [[NCOL, 128], [1, 256]])
                                 ).then_inc(S("SXTA")[0], 16)
                    S("SXTA")[1] += 16
            st["trig_dma"][g] = ("SXTA", S("SXTA")[1])

        def emit_head_front(g):
            slab = XS[g % 2]
            col0 = g * ST
            h = st["h"][g] = {}

            if LEVEL == -4:
                h["trunc"] = st["xload"][g]
                return


            h["ln1"] = ln_stats_slab(L1, slab, st["xload"][g])
            if LEVEL == -23:
                h["trunc"] = h["ln1"]

        def emit_head_trig(g):
            h = st["h"][g]
            if LEVEL == -4:
                return
            for hw in range(2):
                trig_dma = st["trig_dma"].pop(g, None)
                if trig_dma is None:
                    emit_trig_loads(g, hw, ab_tok=st.get("trig_sin"))
                    trig_dma = st["trig_dma"].pop(g)
                sl2 = slice(256 * hw, 256 * hw + 256)

                @block.scalar
                def _(s):
                    wt(s, trig_dma)
                    s.activation(s0_sb[:, sl2], tqa[:], Act.Sin)
                    i1 = s.activation(c1_sb[:, sl2], tqb[:], Act.Sin)
                    inc(i1, "SA")
                s0c1_tok = now("SA")
                st["trig_sin"] = s0c1_tok

                @block.vector
                def _(v):
                    wt(v, s0c1_tok)
                    if hw == 0:
                        wt(v, st["sin_slot"])
                    tok = inc(v.tensor_tensor(tqc[:], tqc[:], c1_sb[:, sl2],
                                              Alu.mult), "SV")
                    wt(v, tok)
                    tok = inc(v.tensor_tensor(sin_r[:, sl2], s0_sb[:, sl2],
                                              tqc[:], Alu.add), "SV")
                    wt(v, tok)
                    tok = inc(v.tensor_tensor(tqd[:], tqd[:], s0_sb[:, sl2],
                                              Alu.mult), "SV")
                    wt(v, tok)
                    i1 = v.tensor_tensor(cos_r[:, sl2], c1_sb[:, sl2],
                                         tqd[:], Alu.subtract)
                    st["trig_scr"] = inc(i1, "SV")
            trig_tok = st["trig_scr"]
            h["trig"] = trig_tok
            if LEVEL == -3:
                h["trunc"] = trig_tok

        def emit_head_back(g):
            h = st["h"][g]
            if "trunc" in h:
                return
            slab = XS[g % 2]
            ln1_tok = h["ln1"]
            gt_last = st["gt_free"]
            u_tok = [None] * NT
            u_evac = [None] * NT

            for t in range(NT):
                def _apu(s, toks, t=t):
                    if t == 0:
                        wt(s, ln1_tok)
                        wt(s, gt_last)
                    i1 = s.activation(GT[:, 1024 * t:1024 * t + 1024],
                                      slab[:, 1024 * t:1024 * t + 1024], Act.Identity,
                                      bias=L1["nm"][:, t:t + 1],
                                      scale=L1["rs"][:, t:t + 1])
                    rinc(i1, toks[0])
                [u_tok[t]] = s_defer(_apu, "SL")
                if t == NT - 1:
                    s_flush()

                if LEVEL == -22:
                    continue

                @block.tensor
                def _(te):
                    wt(te, u_tok[t])
                    if t == 0:
                        wt(te, st["sw"]["ident"])
                    for half in range(2):
                        b = 6 + half
                        bwb(te, b)
                        for ch4 in range(4):
                            ch = 4 * half + ch4
                            i1 = te.transpose(
                                TB[half][:, 128 * ch4:128 * ch4 + 128],
                                GT[:, 1024 * t + 128 * ch:1024 * t + 128 * ch + 128],
                                ident[:])
                        bwe(i1, b, "t")
                    st["gt_free"] = now("SP")

                @block.vector
                def _(v):
                    if t == 0:
                        wt(v, st["ut_free"])
                    for half in range(2):
                        b = 6 + half
                        brb(v, b)
                        i1 = v.tensor_copy(
                            bass.AP(UT, (4 * half) * ST + 128 * t,
                                    [[NCH * ST, 128], [ST, 4], [1, 128]]),
                            TB[half][:, 0:512])
                        bre(i1, b, "v")
                    u_evac[t] = now("SV")
            h["slab_last"] = u_tok[NT - 1]
            if LEVEL == -22:
                h["trunc"] = u_tok[NT - 1]
                return
            h["ut"] = [u_evac[t] for t in range(NT)]

        def emit_head(g):
            emit_head_front(g)
            emit_head_trig(g)
            emit_head_back(g)

        def emit_body(g):
            own = g >= n_pre
            slab = XS[g % 2]
            sl = g % 2
            h = st["h"][g]

            def dbg_store(tok):
                if not own:
                    st["slab_free"][g % 2] = tok
                    return
                sn_store = "SO%d" % (g % 2)

                @block.gpsimd
                def _(sy):
                    wt(sy, tok)
                    row0 = (g - n_pre) * ST
                    sy.dma_start(
                        bass.AP(out_h, row0 * D, [[D, 128], [128 * D, NT], [1, D]]),
                        bass.AP(slab, 0, [[NT * D, 128], [D, NT], [1, D]]),
                    ).then_inc(S(sn_store)[0], 16)
                    S(sn_store)[1] += 16
                st["slab_free"][g % 2] = (sn_store, S(sn_store)[1])

            if "trunc" in h:
                dbg_store(h["trunc"])
                return

            if g + 1 < NSUP:
                emit_xload(g + 1)
                if LEVEL == 3:
                    emit_trig_loads(g + 1, 0)
            trig_tok = h["trig"]
            ut_tok = h["ut"]

            if LEVEL == -2:
                st["ut_free"] = ut_tok[NT - 1]
                dbg_store(ut_tok[NT - 1])
                return

            def proj_rope(dst, wname, free_tok, qsplit=False):
                last_p3 = st["rope_last"]
                silu_toks = [None] * NCH
                for i_ in range(NCH + 4):
                  if i_ < NCH:
                    oc = i_
                    b = oc % 4

                    @block.tensor
                    def _(t):
                        if oc == 0:
                            wt(t, st["sw"][wname])
                            wt(t, st["sw"]["p128"])
                            for tk in ut_tok:
                                wt(t, tk)
                        bwb(t, b)
                        for kc in range(NCH):
                            i1 = t.matmul(pb[b][:, 0:512],
                                          wch(W[wname], kc, 128 * oc, 128),
                                          UT[:, kc * ST:(kc + 1) * ST],
                                          start=(kc == 0), stop=(kc == NCH - 1))
                        bwe(i1, b, "t")

                    def _silu(s, toks, oc=oc, b=b, pw=bw[b],
                              ft=(free_tok if oc == 0 else None)):
                        if ft is not None:
                            wt(s, ft)
                        wt(s, pw)
                        i1 = s.activation(dst[:, oc * ST:(oc + 1) * ST],
                                          pb[b][:, 0:512], Act.Silu)
                        rinc(i1, toks[0])
                    [silu_toks[oc]] = s_defer(_silu, "SL")
                    br[b].append(silu_toks[oc])

                  if i_ >= 4:
                    oc = i_ - 4
                    if True:
                        rb = 4 + oc % 2
                        silu_tok = silu_toks[oc]

                    @block.tensor
                    def _(t):
                        wt(t, silu_tok)
                        bwb(t, rb)
                        i1 = t.matmul(pb[rb][:, 0:512], p128[:],
                                      dst[:, oc * ST:(oc + 1) * ST],
                                      start=True, stop=True)
                        bwe(i1, rb, "t")
                    rot_tok = now("SP")

                    lp = last_p3

                    @block.vector
                    def _(v):
                        if oc == 0:
                            wt(v, trig_tok)
                        wt(v, lp)
                        wt(v, silu_tok)
                        tok = inc(v.tensor_tensor(
                            rope_m1[:], dst[:, oc * ST:(oc + 1) * ST],
                            cos_r[:, 0:ST], Alu.mult), "SV")
                        brb(v, rb)
                        i2 = v.tensor_tensor(rope_m2[:], pb[rb][:, 0:512],
                                             sin_r[:, 0:ST], Alu.mult)
                        tok2 = bre(i2, rb, "v")
                        wt(v, tok)
                        wt(v, tok2)
                        if not qsplit:
                            i3 = v.tensor_tensor(dst[:, oc * ST:(oc + 1) * ST],
                                                 rope_m1[:], rope_m2[:], Alu.add)
                            inc(i3, "SV")
                        else:
                            v.tensor_tensor(QTA[0:64, oc * ST:(oc + 1) * ST],
                                            rope_m1[0:64, :], rope_m2[0:64, :],
                                            Alu.add)
                            tok3 = inc(v.tensor_tensor(
                                QTB[64:128, oc * ST:(oc + 1) * ST],
                                rope_m1[64:128, :], rope_m2[64:128, :], Alu.add), "SV")
                            wt(v, tok3)
                            wt(v, rot_tok)
                            i3 = v.memset(QTB[0:64, oc * ST:(oc + 1) * ST], 0.0)
                            inc(i3, "SV")
                    last_p3 = now("SV")
                s_flush()
                st["rope_last"] = last_p3
                return last_p3

            kt_rope = proj_rope(KT, "wk", st["kt_free"])
            if own and LEVEL >= 1:
                qt_rope = proj_rope(QTB, "wq", st["qt_free"], qsplit=True)
            st["sin_slot"] = now("SV")

            if LEVEL == -1:
                st["ut_free"] = now("SP")
                st["kt_free"] = st["rope_last"]
                st["qt_free"] = now("SP")
                dbg_store(st["rope_last"])
                return

            ktm_tok = [None] * NT

            def emit_ktm(t):
                tb = 6 + t % 2

                @block.tensor
                def _(te):
                    if t == 0:
                        wt(te, kt_rope)
                    wt(te, st["ktm_slot"][t % 2])
                    bwb(te, tb)
                    for ch in range(NCH):
                        i1 = te.transpose(
                            TB[t % 2][:, 128 * ch:128 * ch + 128],
                            KT[:, ch * ST + 128 * t: ch * ST + 128 * t + 128],
                            ident[:])
                    bwe(i1, tb, "t")

                @block.vector
                def _(v):
                    brb(v, tb)
                    i1 = v.tensor_copy(
                        bass.AP(KTM_r, (t % 2) * D, [[2 * D, 128], [1, 1024]]),
                        TB[t % 2][:, 0:1024])
                    bre(i1, tb, "v")
                ktm_tok[t] = now("SV")

            v_tok = [None] * NT
            for t_ in range(NT):
                vA, vB = (2, 3) if t_ % 2 == 0 else (0, 1)

                @block.tensor
                def _(t):
                    if t_ == 0:
                        wt(t, st["sw"]["wv"])
                    bwb(t, vA)
                    bwb(t, vB)
                    for kc in range(NCH):
                        lhs = UT[:, kc * ST + 128 * t_: kc * ST + 128 * t_ + 128]
                        t.matmul(pb[vA][:, 0:512], lhs, wch(W["wv"], kc, 0, 512),
                                 start=(kc == 0), stop=(kc == NCH - 1))
                        i1 = t.matmul(pb[vB][:, 0:512], lhs,
                                      wch(W["wv"], kc, 512, 512),
                                      start=(kc == 0), stop=(kc == NCH - 1))
                    bwe2(i1, vA, vB, "t")

                @block.vector
                def _(v):
                    if t_ == 0:
                        wt(v, st["vs_free"])
                    brb(v, vA)
                    brb(v, vB)
                    v.tensor_copy(VS[:, 1024 * t_:1024 * t_ + 512], pb[vA][:, 0:512])
                    i2 = v.tensor_copy(VS[:, 1024 * t_ + 512:1024 * t_ + 1024],
                                       pb[vB][:, 0:512])
                    bre2(i2, vA, vB, "v")
                v_tok[t_] = now("SV")

            if g == min(0, NSUP - 1) and LEVEL != -4 and not (NO_QUANT
                                                               or NO_QP2):
                emit_c12()
            if not own:
                st["ut_free"] = now("SP")

            if own and LEVEL >= 1:
                @block.tensor
                def _(t):
                    wt(t, st["sw"]["wg1"])
                    wt(t, st["sw"]["wg2"])
                    bwb(t, 5)
                    for kc in range(NCH):
                        i1 = t.matmul(pb[5][0:64, 0:512],
                                      wg1_sb[:, kc * HD:(kc + 1) * HD],
                                      UT[:, kc * ST:(kc + 1) * ST],
                                      start=(kc == 0), stop=(kc == NCH - 1))
                    bwe(i1, 5, "t")

                @block.vector
                def _(v):
                    brb(v, 5)
                    i1 = v.tensor_copy(g1_sb[:], pb[5][0:64, 0:512])
                    bre(i1, 5, "v")
                g1_tok = now("SV")

                for oc in range(NCH):
                    b = oc % 2

                    @block.tensor
                    def _(t):
                        if oc == 0:
                            wt(t, g1_tok)
                        bwb(t, b)
                        i1 = t.matmul(pb[b][:, 0:512],
                                      wg2_sb[:, 128 * oc:128 * oc + 128],
                                      g1_sb[:], start=True, stop=True)
                        bwe(i1, b, "t")

                    def _sig(s, toks, oc=oc, b=b, pw=bw[b],
                             ft=(st["gt_free"] if oc == 0 else None)):
                        if ft is not None:
                            wt(s, ft)
                        wt(s, pw)
                        i2 = s.activation(GT[:, oc * ST:(oc + 1) * ST],
                                          pb[b][:, 0:512], Act.Sigmoid)
                        rinc(i2, toks[0])
                    [sg_tok] = s_defer(_sig, "SL")
                    br[b].append(sg_tok)
                if True:
                    s_flush()

            go_tok = None
            for t in range(NT):
                emit_ktm(t)
                if own and LEVEL >= 1 and LEVEL != 11:
                    msk_tok = [None] * 8
                    ou_base = S("OU")[1]
                    out_tok = [None] * 2
                    ou_tok = [None] * 8
                    cur = 512 * st["sb_cur"]

                    def emit_out(p):
                        if LEVEL == 12:
                            @block.tensor
                            def _(te):
                                wt(te, msk_tok[p])
                                ou_tok[p] = inc(te.nop(), "OU")
                            if p % 4 == 3:
                                out_tok[p // 4] = ou_tok[p]
                            return
                        ob = 4 + p // 4
                        col = 128 * (p % 4)

                        @block.tensor
                        def _(te):
                            if p % 4 == 0:
                                bwb(te, ob)
                            wt(te, msk_tok[p])
                            te.matmul(pb[ob][0:64, col:col + 128],
                                      VS[:, 1024 * t + 128 * p:
                                         1024 * t + 128 * p + 64],
                                      MSK[:, 256 * (p % 4):256 * (p % 4) + 128],
                                      start=True, stop=False, tile_position=(0, 0))
                            if p == 0:
                                wt(te, st["s_chain"])
                            te.matmul(pb[ob][0:64, col:col + 128],
                                      S_sb[:, cur + 64 * p:cur + 64 * p + 64],
                                      QTA[:, p * ST + 128 * t: p * ST + 128 * t + 128],
                                      start=False, stop=True, tile_position=(0, 0))
                            te.matmul(pb[ob][64:128, col:col + 128],
                                      VS[:, 1024 * t + 128 * p + 64:
                                         1024 * t + 128 * p + 128],
                                      MSK[:, 256 * (p % 4) + 128:256 * (p % 4) + 256],
                                      start=True, stop=False, tile_position=(0, 64))
                            i1 = te.matmul(
                                pb[ob][64:128, col:col + 128],
                                S_sb[:, cur + 64 * p:cur + 64 * p + 64],
                                QTB[:, p * ST + 128 * t: p * ST + 128 * t + 128],
                                start=False, stop=True, tile_position=(0, 64))
                            ou_tok[p] = inc(i1, "OU")
                            if p % 4 == 3:
                                out_tok[p // 4] = ou_tok[p]
                                bw[ob] = ou_tok[p]

                    for p in range(8):
                        b = p % 3

                        @block.tensor
                        def _(te):
                            if t == 0 and p == 0:
                                wt(te, kt_rope)
                                wt(te, qt_rope)
                            bwb(te, b)
                            kl = KT[:, p * ST + 128 * t: p * ST + 128 * t + 128]
                            te.matmul(pb[b][:, 0:128], kl,
                                      QTA[:, p * ST + 128 * t: p * ST + 128 * t + 128],
                                      start=True, stop=True)
                            i1 = te.matmul(
                                pb[b][:, 128:256], kl,
                                QTB[:, p * ST + 128 * t: p * ST + 128 * t + 128],
                                start=True, stop=True)
                            bwe(i1, b, "t")

                        @block.vector
                        def _(v):
                            if t == 0 and p == 0 and g == n_pre:
                                wt(v, st["sw"]["maskt"])
                            if p >= 4:
                                wt(v, ("OU", ou_base + (p - 4) + 1))
                            brb(v, b)
                            i1 = v.tensor_tensor(
                                MSK[:, 256 * (p % 4):256 * (p % 4) + 256],
                                pb[b][:, 0:256],
                                bass.AP(maskt, 0, [[128, 128], [0, 2], [1, 128]]),
                                Alu.mult)
                            bre(i1, b, "v")
                        msk_tok[p] = now("SV")
                        if p >= 3:
                            emit_out(p - 3)

                @block.tensor
                def _(te):
                    if not own and t == 0:
                        wt(te, kt_rope)
                    wt(te, ktm_tok[t])
                    wt(te, v_tok[t])
                    bwb(te, 3)
                    for p in range(8):
                        te.matmul(pb[3][0:64, 64 * p:64 * p + 64],
                                  KTM_r[:, (t % 2) * D + 128 * p:
                                        (t % 2) * D + 128 * p + 64],
                                  VS[:, 1024 * t + 128 * p:1024 * t + 128 * p + 64],
                                  start=True, stop=True, tile_position=(0, 0))
                        i1 = te.matmul(pb[3][64:128, 64 * p:64 * p + 64],
                                       KTM_r[:, (t % 2) * D + 128 * p + 64:
                                             (t % 2) * D + 128 * p + 128],
                                       VS[:, 1024 * t + 128 * p + 64:
                                          1024 * t + 128 * p + 128],
                                       start=True, stop=True,
                                       tile_position=(0, 64))
                    st["ktm_slot"][t % 2] = bwe(i1, 3, "t")

                if own and LEVEL >= 1 and LEVEL != 11:
                    for p in range(5, 8):
                        emit_out(p)

                pre_final = (not own) and g == n_pre - 1 and t == NT - 1
                s_war = st["s_war"] if own else st["init"]

                @block.vector
                def _(v):
                    brb(v, 3)
                    i_add = v.tensor_tensor(S_acc[:], S_acc[:], pb[3][:, 0:512],
                                            Alu.add)
                    if own or pre_final:
                        tok = inc(i_add, "SV")
                        wt(v, tok)
                        wt(v, s_war)
                        nxt = 0 if pre_final else 1 - st["sb_cur"]
                        i1 = v.tensor_copy(S_sb[:, 512 * nxt:512 * nxt + 512],
                                           S_acc[:])
                        bre(i1, 3, "v")
                        st["sb_cur"] = nxt
                    else:
                        bre(i_add, 3, "v")
                if own or pre_final:
                    st["s_chain"] = now("SV")
                if own:
                    st["s_war"] = ou_tok[7] if LEVEL >= 1 and LEVEL != 11 else None

                if own and LEVEL >= 1 and LEVEL not in (11, 12):
                    @block.vector
                    def _(v):
                        for half in range(2):
                            ob = 4 + half
                            brb(v, ob)
                            i1 = v.tensor_tensor(
                                bass.AP(GO, 512 * 4 * half + 128 * t,
                                        [[NCH * ST, 128], [ST, 4], [1, 128]]),
                                pb[ob][:, 0:512],
                                bass.AP(GT, 512 * 4 * half + 128 * t,
                                        [[NCH * ST, 128], [ST, 4], [1, 128]]),
                                Alu.mult)
                            bre(i1, ob, "v")
                    go_tok = now("SV")

                if not own and t == 1 and LEVEL == 3 and g + 1 < NSUP:
                    emit_head_front(g + 1)
                    emit_head_trig(g + 1)

            def early_store(tok):
                sn_store = "SO%d" % (g % 2)

                @block.gpsimd
                def _(sy):
                    wt(sy, tok)
                    row0 = (g - n_pre) * ST
                    sy.dma_start(
                        bass.AP(out_h, row0 * D, [[D, 128], [128 * D, NT], [1, D]]),
                        bass.AP(slab, 0, [[NT * D, 128], [D, NT], [1, D]]),
                    ).then_inc(S(sn_store)[0], 16)
                    S(sn_store)[1] += 16
                st["slab_free"][g % 2] = (sn_store, S(sn_store)[1])
                st["ut_free"] = now("SP")
                st["kt_free"] = ktm_tok[NT - 1]
                st["qt_free"] = now("SP")
                st["vs_free"] = st["ktm_slot"][(NT - 1) % 2]
                if own and go_tok is not None:
                    st["gt_free"] = go_tok

            if not own:
                st["slab_free"][g % 2] = h["slab_last"]
                st["kt_free"] = ktm_tok[NT - 1]
                st["vs_free"] = st["ktm_slot"][(NT - 1) % 2]
                if LEVEL == 3 and g + 1 < NSUP:
                    emit_head_back(g + 1)
                return

            if LEVEL <= 1:
                early_store(st["s_chain"])
                return

            st["gt_free"] = go_tok

            @block.tensor
            def _(te):
                wt(te, go_tok)
                bwb(te, 4)
                bwb(te, 5)
                for p in range(NCH):
                    i1 = te.matmul(pb[4][0:1, 0:512], onescol[:],
                                   GO[:, p * ST:(p + 1) * ST],
                                   start=(p == 0), stop=(p == NCH - 1))
                bwe(i1, 4, "t")

            sqslot = [rope_m1[:], rope_m2[:], MSK[:, 0:512], MSK[:, 512:1024]]
            for p in range(NCH):
                def _sqr(s, toks, p=p, gt=(go_tok if p == 0 else None),
                         gq=(("GQ", S("GQ")[1] - 3) if p >= 4 else None)):
                    if gt is not None:
                        wt(s, gt)
                    if gq is not None:
                        wt(s, gq)
                    i1 = s.activation(sqslot[p % 4],
                                      GO[:, p * ST:(p + 1) * ST], Act.Square)
                    rinc(i1, toks[0])
                [sq_t] = s_defer(_sqr, "SQB")

                @block.tensor
                def _(te):
                    wt(te, sq_t)
                    i1 = te.matmul(pb[5][0:1, 0:512], onescol[:],
                                   sqslot[p % 4],
                                   start=(p == 0), stop=(p == NCH - 1))
                    tokq = inc(i1, "GQ")
                    if p == NCH - 1:
                        bw[5] = tokq
            s_flush()

            @block.vector
            def _(v):
                brb(v, 4)
                brb(v, 5)
                tok = inc(v.tensor_copy(zrow[0:1, :], pb[4][0:1, 0:512]), "SV")
                wt(v, tok)
                tok = inc(v.tensor_tensor(zrow[0:1, :], zrow[0:1, :], zrow[0:1, :],
                                          Alu.mult), "SV")
                wt(v, tok)
                i1 = v.tensor_tensor(zrow[0:1, :], pb[5][0:1, 0:512],
                                     zrow[0:1, :], Alu.subtract)
                inc(i1, "SV")
            var_tok = now("SV")

            @block.scalar
            def _(s):
                wt(s, var_tok)
                i1 = s.activation(zrow[0:1, :], zrow[0:1, :],
                                  Act.Ln, bias=eps_sb[0:1, 0:1])
                tok = inc(i1, "SA")
                wt(s, tok)
                i1 = s.activation(rstdb[0:1, :], zrow[0:1, :],
                                  Act.Exp, scale=-0.5)
                inc(i1, "SA")
            srt_tok = now("SA")

            @block.vector
            def _(v):
                wt(v, srt_tok)
                i1 = v.scalar_tensor_tensor(r1[0:1, :], pb[4][0:1, 0:512], -1.0,
                                            rstdb[0:1, :], Alu.mult, Alu.mult)
                r1_tok2 = bre2(i1, 4, 5, "v")
            r1_tok = now("SV")

            @block.tensor
            def _(te):
                wt(te, r1_tok)
                bwb(te, 4)
                i1 = te.matmul(pb[4][:, 0:512], ones1[:], rstdb[:],
                               start=True, stop=True)
                bwe(i1, 4, "t")

            @block.vector
            def _(v):
                brb(v, 4)
                wt(v, st["sw"]["lnw"])
                wt(v, ktm_tok[NT - 1])
                for p in range(NCH):
                    i1 = v.scalar_tensor_tensor(ACORE[:, p * ST:(p + 1) * ST],
                                                GO[:, p * ST:(p + 1) * ST],
                                                lnw_f[:, p:p + 1], pb[4][:, 0:512],
                                                Alu.mult, Alu.mult)
                bre(i1, 4, "v")
            acore_tok = now("SV")
            st["ut_free"] = acore_tok

            res3_tok = [None] * NT
            ln4bn_tok = [None] * NT
            for t in range(NT):
                bA, bB = (0, 1) if t % 2 == 0 else (2, 3)

                @block.tensor
                def _(te):
                    if t == 0:
                        wt(te, acore_tok)
                        wt(te, st["c12_ready"])
                    bwb(te, bA)
                    bwb(te, bB)
                    for kc in range(NCH):
                        lhs = ACORE[:, kc * ST + 128 * t: kc * ST + 128 * t + 128]
                        te.matmul(pb[bA][:, 0:512], lhs, wch(W["wo"], kc, 0, 512),
                                  start=(kc == 0), stop=False)
                        te.matmul(pb[bB][:, 0:512], lhs, wch(W["wo"], kc, 512, 512),
                                  start=(kc == 0), stop=False)
                    rl = r1[0:1, 128 * t:128 * t + 128]
                    on = ones1[0:1, 0:128]
                    te.matmul(pb[bA][:, 0:512], rl, c1r[0:1, 0:512],
                              start=False, stop=False)
                    te.matmul(pb[bA][:, 0:512], on, c2r[0:1, 0:512],
                              start=False, stop=True)
                    te.matmul(pb[bB][:, 0:512], rl, c1r[0:1, 512:1024],
                              start=False, stop=False)
                    i1 = te.matmul(pb[bB][:, 0:512], on, c2r[0:1, 512:1024],
                                   start=False, stop=True)
                    bwe2(i1, bA, bB, "t")

                ln3_tok = ln_banks_apply(L3, t, bA, bB)

                @block.vector
                def _(v):
                    wt(v, ln3_tok)
                    v.tensor_tensor(slab[:, 1024 * t:1024 * t + 512],
                                    slab[:, 1024 * t:1024 * t + 512],
                                    pb[bA][:, 0:512], Alu.add)
                    i1 = v.tensor_tensor(slab[:, 1024 * t + 512:1024 * t + 1024],
                                         slab[:, 1024 * t + 512:1024 * t + 1024],
                                         pb[bB][:, 0:512], Alu.add)
                    bre2(i1, bA, bB, "v")
                    res3_tok[t] = now("SV")
                    wt(v, res3_tok[t])
                    v.bn_stats(L4["bn"][:, 12 * t:12 * t + 6],
                               slab[:, 1024 * t:1024 * t + 512])
                    i1 = v.bn_stats(L4["bn"][:, 12 * t + 6:12 * t + 12],
                                    slab[:, 1024 * t + 512:1024 * t + 1024])
                    ln4bn_tok[t] = inc(i1, "SV")

            s_flush()

            if LEVEL <= 2:
                early_store(res3_tok[NT - 1])
                return

            @block.vector
            def _(v):
                for t in range(NT):
                    wt(v, ln4bn_tok[t])
                    last = v.bn_aggr(L4["mv"][:, 2 * t:2 * t + 2],
                                     L4["bn"][:, 12 * t:12 * t + 12])
                inc(last, "SV")
            l4stats_tok = now("SV")

            @block.scalar
            def _(s):
                wt(s, l4stats_tok)
                i1 = s.activation(L4["sq"][:],
                                  bass.AP(L4["mv"], 1, [[2 * NT, 128], [2, NT]]),
                                  Act.Ln, bias=eps_sb[:, 0:1])
                tok = inc(i1, "SA")
                wt(s, tok)
                i1 = s.activation(L4["rs"][:], L4["sq"][:], Act.Exp, scale=-0.5)
                inc(i1, "SA")
            l4sq_tok = now("SA")

            @block.vector
            def _(v):
                wt(v, l4sq_tok)
                i1 = v.scalar_tensor_tensor(
                    L4["nm"][:], bass.AP(L4["mv"], 0, [[2 * NT, 128], [2, NT]]),
                    -1.0, L4["rs"][:], Alu.mult, Alu.mult)
                inc(i1, "SV")
            ln4_tok = now("SV")
            t4_tok = [None] * NT

            @block.scalar
            def _(s):
                wt(s, ln4_tok)
                for t in range(NT):
                    i1 = s.activation(TS[:, 1024 * t:1024 * t + 1024],
                                      slab[:, 1024 * t:1024 * t + 1024], Act.Identity,
                                      bias=L4["nm"][:, t:t + 1],
                                      scale=L4["rs"][:, t:t + 1])
                    t4_tok[t] = inc(i1, "SA")

            tt_evac = [None] * NT
            for t in range(NT):
                @block.tensor
                def _(te):
                    wt(te, t4_tok[t])
                    for half in range(2):
                        b = 6 + half
                        bwb(te, b)
                        for ch4 in range(4):
                            ch = 4 * half + ch4
                            i1 = te.transpose(
                                TB[half][:, 128 * ch4:128 * ch4 + 128],
                                TS[:, 1024 * t + 128 * ch: 1024 * t + 128 * ch + 128],
                                ident[:])
                        bwe(i1, b, "t")

                @block.vector
                def _(v):
                    for half in range(2):
                        b = 6 + half
                        brb(v, b)
                        i1 = v.tensor_copy(
                            bass.AP(TTs, (4 * half) * ST + 128 * t,
                                    [[NCH * ST, 128], [ST, 4], [1, 128]]),
                            TB[half][:, 0:512])
                        bre(i1, b, "v")
                tt_evac[t] = now("SV")
            tt_tok = tt_evac[NT - 1]

            @block.tensor
            def _(te):
                wt(te, tt_tok)
                wt(te, st["sw"]["fa"])
                bwb(te, 4)
                for kc in range(NCH):
                    i1 = te.matmul(pb[4][0:16, 0:512], fa_sb[:, kc * 16:(kc + 1) * 16],
                                   TTs[:, kc * ST:(kc + 1) * ST],
                                   start=(kc == 0), stop=(kc == NCH - 1))
                bwe(i1, 4, "t")

            @block.vector
            def _(v):
                brb(v, 4)
                i1 = v.tensor_copy(g1_sb[0:16, :], pb[4][0:16, 0:512])
                bre(i1, 4, "v")
            l1_tok = now("SV")

            if LEVEL == 3 and g + 1 < NSUP:
                emit_head_front(g + 1)
                emit_head_back(g + 1)

            res5_tok = [None] * NT
            for t in range(NT):
                bA, bB = (0, 1) if t % 2 == 0 else (2, 3)

                @block.tensor
                def _(te):
                    if t == 0:
                        wt(te, st["wf_ready"])
                        wt(te, st["sw"]["fbias"])
                        wt(te, st["sw"]["fb"])
                        wt(te, l1_tok)
                    bwb(te, bA)
                    bwb(te, bB)
                    for kc in range(NCH):
                        lhs = TTs[:, kc * ST + 128 * t: kc * ST + 128 * t + 128]
                        te.matmul(pb[bA][:, 0:512], lhs, wch(W["wf"], kc, 0, 512),
                                  start=(kc == 0), stop=False)
                        te.matmul(pb[bB][:, 0:512], lhs, wch(W["wf"], kc, 512, 512),
                                  start=(kc == 0), stop=False)
                    te.matmul(pb[bA][:, 0:512], ones1[:], fbias_sb[0:1, 0:512],
                              start=False, stop=False)
                    te.matmul(pb[bB][:, 0:512], ones1[:], fbias_sb[0:1, 512:1024],
                              start=False, stop=False)
                    ll = g1_sb[0:16, 128 * t:128 * t + 128]
                    te.matmul(pb[bA][:, 0:512], ll, fb_sb[:, 0:512],
                              start=False, stop=True)
                    i1 = te.matmul(pb[bB][:, 0:512], ll, fb_sb[:, 512:1024],
                                   start=False, stop=True)
                    bwe2(i1, bA, bB, "t")

                ln5_tok = ln_banks_apply(L5, t, bA, bB)

                @block.vector
                def _(v):
                    wt(v, ln5_tok)
                    v.tensor_tensor(slab[:, 1024 * t:1024 * t + 512],
                                    slab[:, 1024 * t:1024 * t + 512],
                                    pb[bA][:, 0:512], Alu.add)
                    i1 = v.tensor_tensor(slab[:, 1024 * t + 512:1024 * t + 1024],
                                         slab[:, 1024 * t + 512:1024 * t + 1024],
                                         pb[bB][:, 0:512], Alu.add)
                    bre2(i1, bA, bB, "v")
                    res5_tok[t] = now("SV")

            s_flush()
            if LEVEL == 3 and g + 1 < NSUP:
                emit_head_trig(g + 1)

            sn_store = "SO%d" % (g % 2)

            @block.gpsimd
            def _(sy):
                for t in range(NT):
                    wt(sy, res5_tok[t])
                row0 = (g - n_pre) * ST
                sy.dma_start(
                    bass.AP(out_h, row0 * D, [[D, 128], [128 * D, NT], [1, D]]),
                    bass.AP(slab, 0, [[NT * D, 128], [D, NT], [1, D]]),
                ).then_inc(S(sn_store)[0], 16)
                S(sn_store)[1] += 16
            st["slab_free"][g % 2] = (sn_store, S(sn_store)[1])

            st["kt_free"] = now("SP")
            st["qt_free"] = now("SP")
            st["vs_free"] = tt_tok

            if g < 4 and LEVEL != -4 and not (NO_QUANT or NO_QP2):
                emit_qp2_loads(8)
            if LEVEL != 3 and LEVEL != -4 and not (NO_QUANT or NO_QP2):
                emit_qp2_ladder(8)

        emit_head(0)
        for g in range(NSUP):
            if g == n_pre and LEVEL != -4 and not (NO_QUANT or NO_QP2):
                emit_qp2_loads(32)
                emit_qp2_ladder(32)
            emit_body(g)
            if LEVEL != 3 and g + 1 < NSUP:
                emit_head(g + 1)
            if LEVEL == 3 and not (NO_QUANT or NO_QP2):
                emit_qp2_ladder(8)

        @block.gpsimd
        def _(sy):
            sy.wait_ge(S("SO0")[0], S("SO0")[1])
            sy.wait_ge(S("SO1")[0], S("SO1")[1])

    return nc


def _trig_tables(pos):
    import jax
    import jax.numpy as jnp
    d = 64
    with jax.default_device(jax.devices("cpu")[0]):
        j32 = jnp.arange(0, d, 2, dtype=jnp.float32)
        inv = 1.0 / 10000.0 ** (j32 / d)
        pos32 = jnp.asarray(np.asarray(pos), dtype=jnp.float32)
        ang = pos32[:, None] * inv[None, :]
        s32 = np.asarray(jnp.sin(ang))
        c32 = np.asarray(jnp.cos(ang))
    su = s32.astype(np.float64)
    cu = c32.astype(np.float64)
    r_hi = np.arcsin(np.clip(su, -1.0, 1.0)).astype(np.float32)
    u_hi = np.arcsin(np.clip(cu, -1.0, 1.0)).astype(np.float32)
    s0 = np.sin(r_hi.astype(np.float64))
    c1 = np.sin(u_hi.astype(np.float64))
    with np.errstate(divide="ignore", invalid="ignore"):
        r_lo = np.where(c1 != 0.0, (su - s0) / c1, 0.0).astype(np.float32)
        w = np.where(s0 != 0.0, (c1 - cu) / s0, 0.0).astype(np.float32)
    u_lo = None

    sig = np.ones(64, np.float32)
    sig[:32] = -1.0
    T = len(pos)

    def fm(tab, signed):
        full = np.empty((128, T), np.float32)
        for p in range(128):
            j = p % 64
            v = tab[:, j % 32]
            if signed:
                v = v * sig[j]
            full[p] = v
        return full

    return {"trh": fm(r_hi, True), "trl": fm(r_lo, True),
            "tuh": fm(u_hi, False), "tw": fm(w, True)}


_PROG_CACHE = {}


def _get_prog(n_pre, n_own):
    key = (n_pre, n_own)
    if key not in _PROG_CACHE:
        _PROG_CACHE[key] = build_program(n_pre, n_own)
    return _PROG_CACHE[key]


def make_in_maps(inputs, n_pre, n_own, n_batches):
    x = np.ascontiguousarray(np.asarray(inputs["L0_attn_x"], np.float32))
    pre_tok, own_tok = n_pre * ST, n_own * ST
    shared = {
        "wq": np.ascontiguousarray(inputs["L0_attn_Wq"], np.float32),
        "wk": np.ascontiguousarray(inputs["L0_attn_Wk"], np.float32),
        "wv": np.ascontiguousarray(inputs["L0_attn_Wv"], np.float32),
        "wo": np.ascontiguousarray(inputs["L0_attn_Wo"], np.float32),
        "wg1": np.ascontiguousarray(inputs["L0_attn_Wg1"], np.float32),
        "wg2": np.ascontiguousarray(inputs["L0_attn_Wg2"], np.float32),
        "lnw": np.ascontiguousarray(
            np.asarray(inputs["L0_attn_ln_w"], np.float32).reshape(NCH, 128).T),
        "lnb": np.ascontiguousarray(
            np.asarray(inputs["L0_attn_ln_b"], np.float32).reshape(NCH, 128).T),
        "wf": np.ascontiguousarray(inputs["L0_ffn_kernel"], np.float32),
        "fbias": np.ascontiguousarray(inputs["L0_ffn_bias"], np.float32),
        "fa": np.ascontiguousarray(inputs["L0_ffn_a"], np.float32),
        "fb": np.ascontiguousarray(inputs["L0_ffn_b"], np.float32),
    }
    kq = np.arange(128)
    shared["maskt"] = (kq[:, None] <= kq[None, :]).astype(np.float32)
    p128_m = np.zeros((128, 128), np.float32)
    for p in range(128):
        blk, j = p // 64 * 64, p % 64
        p128_m[p, blk + (j + 32) % 64] = 1.0
    shared["p128"] = p128_m
    shared["ident"] = np.eye(128, dtype=np.float32)

    trig_pre = _trig_tables(np.arange(pre_tok, dtype=np.float64)) if pre_tok else None
    in_maps = []
    for b in range(n_batches):
        for half in range(2):
            own0 = half * own_tok
            tr_own = _trig_tables(own0 + np.arange(own_tok, dtype=np.float64))
            m = dict(shared)
            m["x_own"] = np.ascontiguousarray(x[b, own0:own0 + own_tok])
            if half == 1 and pre_tok:
                m["x_pre"] = np.ascontiguousarray(x[b, 0:pre_tok])
            else:
                m["x_pre"] = np.zeros((max(pre_tok, ST), D), np.float32)
            for n in ("trh", "trl", "tuh", "tw"):
                if pre_tok:
                    m[n] = np.ascontiguousarray(
                        np.concatenate([trig_pre[n], tr_own[n]], axis=1))
                else:
                    m[n] = np.ascontiguousarray(tr_own[n])
            in_maps.append(m)
    return in_maps


def kernel(**inputs):
    n_pre, n_own = 4, 4
    x = np.asarray(inputs["L0_attn_x"])
    B, N, _ = x.shape
    in_maps = make_in_maps(inputs, n_pre, n_own, B)
    nc = _get_prog(n_pre, n_own)
    res = run_bass_kernel_spmd(nc, in_maps, core_ids=list(range(len(in_maps))))
    out = np.empty((B, N, D), np.float32)
    own_tok = n_own * ST
    for b in range(B):
        for half in range(2):
            shard = np.asarray(res.results[2 * b + half]["out"])
            out[b, half * own_tok:(half + 1) * own_tok] = shard
    return out
